# version 15
# baseline (speedup 1.0000x reference)
import numpy as np
import ml_dtypes
from contextlib import ExitStack

import concourse.bass as bass
import concourse.mybir as mybir
from concourse.bass_utils import run_bass_kernel_spmd

F32 = mybir.dt.float32
BF16 = mybir.dt.bfloat16
AF = mybir.ActivationFunctionType
ALU = mybir.AluOpType
AX = mybir.AxisListType

NCORES = 8
TOK = 2048
D = 4096
NCH = 32
NH = 16
LK = 8192
MCH = LK // 128
HALF = 1024
CIN_SH = D // NCORES
EPS = 1e-5
SCALE = 1.0 / np.sqrt(256.0)

R8 = [[0, 1, 2, 3, 4, 5, 6, 7]]
R4 = [[0, 1, 2, 3], [4, 5, 6, 7]]

_CACHE = {}


def _build_tables():
    g = [1.0, 1.0, 1.0]
    dim, n = 3, 8
    masks = sorted(range(n), key=lambda m: (bin(m).count("1"), m))
    inv = {m: i for i, m in enumerate(masks)}
    C = np.zeros((n, n, n), dtype=np.float32)
    for i, a in enumerate(masks):
        for j, b in enumerate(masks):
            sw, at = 0, a >> 1
            while at:
                sw += bin(at & b).count("1")
                at >>= 1
            s = -1.0 if sw & 1 else 1.0
            for t in range(dim):
                if a >> t & 1 and b >> t & 1:
                    s *= float(g[t])
            C[i, j, inv[a ^ b]] = s
    grades = np.array([bin(m).count("1") for m in masks])
    rev = np.where(grades * (grades - 1) // 2 % 2 == 1, -1.0, 1.0).astype(np.float32)
    met = np.array([float(np.prod([g[t] for t in range(dim) if m >> t & 1] + [1.0]))
                    for m in masks], dtype=np.float32)
    return C, rev, met


def _g_const():
    C, rev, met = _build_tables()
    bivk = C[:, :, [4, 5, 6]]
    G = np.zeros((4, 128, 128), dtype=np.float32)
    for d in range(16):
        for i in range(8):
            for j in range(8):
                for c in range(3):
                    G[c, d * 8 + i, d * 8 + j] = rev[i] * bivk[i, j, c]
            G[3, d * 8 + i, d * 8 + i] = SCALE * rev[i] * met[i]
    return G


class Ch:
    __slots__ = ("sem", "n", "name")

    def __init__(self, sem, name):
        self.sem = sem
        self.n = 0
        self.name = name

    def bump(self, k=1):
        self.n += k
        return self.n


class K:

    def __init__(self, nc):
        self.nc = nc
        self.rec = {e: [] for e in ("y", "t", "v", "s", "g")}
        self._chs = {}
        self.owner = {}
        self._burst = {e: None for e in ("y", "t", "v", "s", "g")}
        self.selfch = {"v": self.ch("sv"), "s": self.ch("ss")}

    def ch(self, name):
        if name not in self._chs:
            self._chs[name] = Ch(self.nc.alloc_semaphore(name), name)
        return self._chs[name]

    def _raw(self, eng, fn):
        self.rec[eng].append(fn)

    def on(self, eng, fn):
        self._burst[eng] = None
        self.rec[eng].append(fn)

    def wait(self, eng, tok):
        if tok is None:
            return
        def f(e, t=tok):
            v = int(t[1])
            if v > 0:
                e.wait_ge(t[0].sem, v)
        self.on(eng, f)

    def waitv(self, eng, ch, val):
        b = None
        for e_, bb in self._burst.items():
            if bb is not None and bb[0] is ch:
                b = bb
                break
        if b is not None and int(val) == ch.n:
            self.wait(eng, b[1][-1])
        else:
            self.wait(eng, [ch, val])

    def op(self, eng, fn, ch=None, k=1):
        if eng in ("v", "s"):
            assert ch is None
            sc = self.selfch[eng]
            if sc.n > 0:
                def w(e, s=sc.sem, v=sc.n):
                    e.wait_ge(s, v)
                self.on(eng, w)
            else:
                self._burst[eng] = None
            v2 = sc.bump(1)
            self._raw(eng, lambda e, f=fn, s=sc.sem: f(e).then_inc(s, 1))
            return [sc, v2]
        if ch is None:
            self.on(eng, fn)
            return None
        self.owner.setdefault(ch.name, eng)
        self._burst[eng] = None
        v2 = ch.bump(k)
        self._raw(eng, lambda e, f=fn, s=ch.sem, k=k: f(e).then_inc(s, k))
        return [ch, v2]

    def dma(self, eng, out, in_, ch, **kw):
        self.owner.setdefault(ch.name, eng)
        tok = [ch, ch.bump(16)]
        b = self._burst[eng]
        if b is not None and b[0] is ch:
            b[1].append(tok)
            for t in b[1]:
                t[1] = ch.n
        else:
            self._burst[eng] = (ch, [tok])
        self._raw(eng, lambda e, o=out, i=in_, kw=kw, s=ch.sem:
                  e.dma_start(out=o, in_=i, **kw).then_inc(s, 16))
        return tok

    def barrier(self):
        ph = self.ch("ph")
        drains = {e: [] for e in ("y", "t", "v", "s", "g")}
        for name, eng in self.owner.items():
            if name.startswith("cc_"):
                continue
            c = self._chs[name]
            if c.n > 0:
                drains[eng].append((c.sem, c.n))
        for e_ in ("v", "s"):
            sc = self.selfch[e_]
            if sc.n > 0:
                drains[e_].append((sc.sem, sc.n))
        tgt = ph.n + 5
        ph.n = tgt
        for eng in ("y", "t", "v", "s", "g"):
            def f(e, ds=tuple(drains[eng]), s=ph.sem, t=tgt):
                for sem_, v_ in ds:
                    e.wait_ge(sem_, v_)
                e.nop().then_inc(s, 1)
                e.wait_ge(s, t)
            self.on(eng, f)


def build_nc():
    nc = bass.Bass("TRN2", target_bir_lowering=False, debug=False)
    k = K(nc)
    sv = k.selfch["v"]
    ss = k.selfch["s"]
    warm_ctx = {"ps": None, "ones": None}

    def warm():
        ps = warm_ctx["ps"]
        if ps is None:
            return
        on = warm_ctx["ones"]
        k.op("t", lambda e, ps=ps, on=on: e.matmul(ps[0:16, 0:16], on[0:1, 0:16],
                                                   on[0:1, 0:16], start=True, stop=True))

    x_in = nc.dram_tensor("x", [TOK, D], F32, kind="ExternalInput")
    wq_sl = nc.dram_tensor("wq_sl", [D, CIN_SH], F32, kind="ExternalInput")
    wk_sl = nc.dram_tensor("wk_sl", [D, CIN_SH], F32, kind="ExternalInput")
    wv_sl = nc.dram_tensor("wv_sl", [D, CIN_SH], F32, kind="ExternalInput")
    wf_sl = nc.dram_tensor("wf_sl", [CIN_SH, D], F32, kind="ExternalInput")
    bq_in = nc.dram_tensor("bq", [1, D], F32, kind="ExternalInput")
    bk_in = nc.dram_tensor("bk", [1, D], F32, kind="ExternalInput")
    bv_in = nc.dram_tensor("bv", [1, D], F32, kind="ExternalInput")
    bw_in = nc.dram_tensor("bw", [1, NH * 3], F32, kind="ExternalInput")
    out_d = nc.dram_tensor("out", [TOK, D], F32, kind="ExternalOutput")

    wqt_loc = nc.dram_tensor("wqt_loc", [CIN_SH, D], BF16)
    wkt_loc = nc.dram_tensor("wkt_loc", [CIN_SH, D], BF16)
    wvt_loc = nc.dram_tensor("wvt_loc", [CIN_SH, D], BF16)
    wf_loc = nc.dram_tensor("wf_loc", [CIN_SH, D], BF16)
    wqt = nc.dram_tensor("wqt", [D, D], BF16, addr_space="Shared")
    wkt = nc.dram_tensor("wkt", [D, D], BF16, addr_space="Shared")
    wvt = nc.dram_tensor("wvt", [D, D], BF16, addr_space="Shared")
    wf16 = nc.dram_tensor("wf16", [D, D], BF16, addr_space="Shared")
    h2t_d = nc.dram_tensor("h2t", [D, TOK], BF16)
    x1_d = nc.dram_tensor("x1", [TOK, D], F32)
    x2_d = nc.dram_tensor("x2", [TOK, D], F32)
    q_loc = nc.dram_tensor("q_loc", [D, TOK], BF16)
    k_loc = nc.dram_tensor("k_loc", [2 * D, HALF], BF16)
    v_loc = nc.dram_tensor("v_loc", [2 * NH * HALF, 256], BF16)
    k_full = [nc.dram_tensor("k_full%d" % i, [4 * D, HALF], BF16) for i in range(2)]
    v_full = [nc.dram_tensor("v_full%d" % i, [4 * NH * HALF, 256], BF16) for i in range(2)]
    kn_loc = nc.dram_tensor("kn_loc", [1, NH], F32)
    kn_g = nc.dram_tensor("kn_g", [4, NH], F32)

    ident16_h = nc.inline_tensor(np.eye(128, dtype=ml_dtypes.bfloat16), name="ident16")
    identf_h = nc.inline_tensor(np.eye(128, dtype=np.float32), name="identf")
    g_h = nc.inline_tensor(_g_const(), name="gconst")

    top = ExitStack()
    SBt = lambda n_, s_, d_=F32: top.enter_context(nc.sbuf_tensor(n_, s_, d_))
    ident16 = SBt("ident16s", [128, 128], BF16)
    identf = SBt("identfs", [128, 128], F32)
    ones_f = SBt("ones_f", [1, 128], F32)
    ones_bf = SBt("ones_bf", [1, 128], BF16)
    onescol_bf = SBt("onescol_bf", [128, 1], BF16)
    bias_q = SBt("bias_q", [128, NCH], F32)
    bias_k = SBt("bias_k", [128, NCH], F32)
    bv16 = SBt("bv16", [1, D], BF16)
    expb = SBt("expb", [128, NH], F32)
    epsc = SBt("epsc", [128, 1], F32)
    warm_ctx["ones"] = ones_bf

    ld_misc = k.ch("ld_misc")
    st_n = k.ch("st_n")
    mm = k.ch("mm")
    cc_wq = k.ch("cc_wq")
    cc_wk = k.ch("cc_wk")
    cc_wv = k.ch("cc_wv")
    cc_wf = k.ch("cc_wf")
    cc_k = [k.ch("cc_k0"), k.ch("cc_k1")]
    cc_v = [k.ch("cc_v0"), k.ch("cc_v1")]
    cc_n = k.ch("cc_n")

    with ExitStack() as p0:
        sbal = lambda n_, s_, d_=F32: p0.enter_context(nc.sbuf_tensor(n_, s_, d_))
        psal = lambda n_, s_, d_=F32: p0.enter_context(nc.psum_tensor(n_, s_, d_))
        g_sb = sbal("g_sb", [128, 4 * 128])
        m_sb = sbal("m_sb", [128, NH * 128])
        tmp_sb = sbal("tmp_sb", [128, 128])
        bwb = sbal("bwb", [128, 48])
        bqraw = sbal("bqraw", [128, NCH])
        bvrow = sbal("bvrow", [1, D])
        bwrow = sbal("bwrow", [1, 48])
        ps_bw = psal("ps_bw", [128, 64])
        ps_b = [psal("ps_b0", [128, 64]), psal("ps_b1", [128, 64])]

        k.dma("y", ident16[:], ident16_h.ap(), ld_misc)
        k.dma("y", identf[:], identf_h.ap(), ld_misc)
        k.dma("y", g_sb.ap().rearrange("p (c n) -> p c n", n=128),
              g_h.ap().rearrange("c p n -> p c n"), ld_misc)
        k.dma("y", bwrow[:], bw_in.ap(), ld_misc)
        k.dma("y", bvrow[:], bv_in.ap(), ld_misc)
        bqrow = sbal("bqrow", [1, D])
        bkrow = sbal("bkrow", [1, D])
        k.dma("y", bqrow[:], bq_in.ap(), ld_misc)
        tok_ld0 = k.dma("y", bkrow[:], bk_in.ap(), ld_misc)

        k.wait("v", tok_ld0)
        k.op("v", lambda e: e.memset(ones_f[:], 1.0))
        k.op("v", lambda e: e.memset(ones_bf[:], 1.0))
        k.op("v", lambda e: e.memset(onescol_bf[:], 1.0))
        k.op("v", lambda e: e.memset(epsc[:], float(EPS)))
        v0 = k.op("v", lambda e: e.tensor_copy(bv16[:], bvrow[:]))

        k.wait("t", tok_ld0)
        k.wait("t", v0)
        t0 = k.op("t", lambda e: e.matmul(ps_bw[:, 0:48], ones_f[:], bwrow[:],
                                          start=True, stop=True), mm)
        tcols = None
        for c in range(NCH):
            k.op("t", lambda e, c=c: e.matmul(
                ps_b[0].ap()[:, c:c + 1], bqrow.ap()[0:1, c * 128:(c + 1) * 128],
                ones_f.ap()[0:1, 0:1], start=True, stop=True))
            tcols = k.op("t", lambda e, c=c: e.matmul(
                ps_b[1].ap()[:, c:c + 1], bkrow.ap()[0:1, c * 128:(c + 1) * 128],
                ones_f.ap()[0:1, 0:1], start=True, stop=True), mm)
        k.wait("v", tcols)
        k.op("v", lambda e: e.tensor_copy(bqraw[:], ps_b[0].ap()[:, 0:NCH]))
        vbk = k.op("v", lambda e: e.tensor_copy(bias_k[:], ps_b[1].ap()[:, 0:NCH]))
        k.wait("v", t0)
        k.op("v", lambda e: e.tensor_copy(bwb[:, 0:48], ps_bw[:, 0:48]))
        vM = None
        for h in range(NH):
            mh = m_sb.ap()[:, h * 128:(h + 1) * 128]
            g0 = g_sb.ap()[:, 0:128]
            g1 = g_sb.ap()[:, 128:256]
            g2 = g_sb.ap()[:, 256:384]
            g3 = g_sb.ap()[:, 384:512]
            s0 = bwb.ap()[:, h * 3:h * 3 + 1]
            s1 = bwb.ap()[:, h * 3 + 1:h * 3 + 2]
            s2 = bwb.ap()[:, h * 3 + 2:h * 3 + 3]
            k.op("v", lambda e, g0=g0, s0=s0: e.tensor_scalar_mul(tmp_sb[:], g0, s0))
            k.op("v", lambda e, mh=mh, g3=g3: e.tensor_add(mh, tmp_sb[:], g3))
            k.op("v", lambda e, g1=g1, s1=s1: e.tensor_scalar_mul(tmp_sb[:], g1, s1))
            k.op("v", lambda e, mh=mh: e.tensor_add(mh, mh, tmp_sb[:]))
            k.op("v", lambda e, g2=g2, s2=s2: e.tensor_scalar_mul(tmp_sb[:], g2, s2))
            vM = k.op("v", lambda e, mh=mh: e.tensor_add(mh, mh, tmp_sb[:]))

        k.wait("t", vM)
        k.wait("t", vbk)
        sq_toks = [None, None]
        for c in range(NCH):
            h = c // 2
            pb = ps_b[c % 2]
            k.wait("t", sq_toks[c % 2])
            tq = k.op("t", lambda e, pb=pb, h=h, c=c: e.matmul(
                pb[:, 0:1], m_sb.ap()[:, h * 128:(h + 1) * 128], bqraw.ap()[:, c:c + 1],
                start=True, stop=True), mm)
            k.wait("s", tq)
            sq_toks[c % 2] = k.op("s", lambda e, pb=pb, c=c: e.copy(bias_q[:, c:c + 1],
                                                                    pb[:, 0:1]))

        ld_w = [k.ch("ld_w%d" % i) for i in range(4)]
        st_w = [k.ch("st_w%d" % i) for i in range(4)]
        ld_f = [k.ch("ld_f0"), k.ch("ld_f1")]
        st_f = [k.ch("st_f0"), k.ch("st_f1")]
        cons_w = k.ch("cons_w")
        wblk = [sbal("wblk%d" % i, [128, 1024]) for i in range(4)]
        wst = [p0.enter_context(nc.sbuf_tensor("wst%d" % i, [128, 1024], BF16))
               for i in range(2)]
        fblk = [sbal("fblk0", [128, D]), sbal("fblk1", [128, D])]
        fst = [p0.enter_context(nc.sbuf_tensor("fst0", [128, D], BF16)),
               p0.enter_context(nc.sbuf_tensor("fst1", [128, D], BF16))]
        ps_wp = [psal("ps_wp%d" % i, [128, 1024]) for i in range(2)]
        warm_ctx["ps"] = psal("ps_warm1", [16, 16])

        fev_toks = [None, None]
        fst_toks = [None, None]
        for j in range(CIN_SH // 128):
            b = j % 2
            k.wait("y", fev_toks[b])
            lw = k.dma("y", fblk[b][:], wf_sl.ap()[j * 128:(j + 1) * 128, :], ld_f[b])
            k.wait("v", lw)
            k.wait("v", fst_toks[b])
            fev_toks[b] = k.op("v", lambda e, b=b: e.tensor_copy(fst[b][:], fblk[b][:]))
            k.wait("g", fev_toks[b])
            fst_toks[b] = k.dma("g", wf_loc.ap()[j * 128:(j + 1) * 128, :], fst[b][:],
                                st_f[b])
        k.waitv("g", st_f[0], st_f[0].n)
        k.waitv("g", st_f[1], st_f[1].n)
        k.op("g", lambda e: e.collective_compute(
            "AllGather", ALU.bypass, replica_groups=R8,
            ins=[wf_loc.ap().opt()], outs=[wf16.ap().opt()]), cc_wf)

        idx = 0
        ev_toks = [None] * 2
        st_toks = [None] * 2
        for wsrc, wdst, rhs_is_m, ccx in ((wq_sl, wqt_loc, True, cc_wq),
                                          (wk_sl, wkt_loc, False, cc_wk),
                                          (wv_sl, wvt_loc, False, cc_wv)):
            srcw2 = wsrc.ap().rearrange("(cc two p) n -> p cc two n", p=128, two=2)
            for c2 in range(NCH // 2):
                bw = idx % 4
                bp = idx % 2
                if idx >= 4:
                    k.waitv("y", cons_w, 8 * (idx - 3))
                lw = k.dma("y", wblk[bw].ap().rearrange("p (two n) -> p two n", two=2),
                           srcw2[:, c2], ld_w[bw])
                k.wait("t", lw)
                k.wait("t", ev_toks[bp])
                tm = None
                for two in range(2):
                    c = 2 * c2 + two
                    rhs = (m_sb.ap()[:, (c // 2) * 128:(c // 2 + 1) * 128]
                           if rhs_is_m else identf.ap())
                    for j in range(CIN_SH // 128):
                        tm = k.op("t", lambda e, bw=bw, bp=bp, j=j, two=two,
                                  rhs=rhs: e.matmul(
                            ps_wp[bp].ap()[:, two * 512 + j * 128: two * 512 + (j + 1) * 128],
                            wblk[bw].ap()[:, two * 512 + j * 128: two * 512 + (j + 1) * 128],
                            rhs, start=True, stop=True), cons_w)
                k.wait("v", tm)
                k.wait("v", st_toks[bp])
                ev_toks[bp] = k.op("v", lambda e, bp=bp: e.tensor_copy(wst[bp][:],
                                                                       ps_wp[bp][:]))
                k.wait("g", ev_toks[bp])
                dstap = wdst.ap().rearrange("(j p) n -> p j n", p=128)
                srcap = wst[bp].ap().rearrange("p (two j n) -> p two j n", two=2, n=128)
                for two in range(2):
                    c = 2 * c2 + two
                    st_toks[bp] = k.dma("g", dstap[:, :, c * 128:(c + 1) * 128],
                                        srcap[:, two], st_w[bp])
                idx += 1
            for chx in st_w:
                k.waitv("g", chx, chx.n)
            k.op("g", lambda e, a_=wsrc, w1=wdst, w2={"wqt_loc": wqt, "wkt_loc": wkt,
                 "wvt_loc": wvt}[wdst.name]: e.collective_compute(
                "AllGather", ALU.bypass, replica_groups=R8,
                ins=[w1.ap().opt()], outs=[w2.ap().opt()]), ccx)

    k.barrier()

    ld_x = [k.ch("ld_x0"), k.ch("ld_x1b")]
    ld_wr = [k.ch("ld_wr0"), k.ch("ld_wr1")]
    ld_h2 = k.ch("ld_h2")
    st_h = k.ch("st_h")
    tr = k.ch("tr")
    ld_xs = [k.ch("ld_xs0"), k.ch("ld_xs1")]
    st_x1 = [k.ch("st_x10"), k.ch("st_x11")]
    mm_f = k.ch("mm_f")

    def ln_front_a(st, src_d, half, tt):
        xb, hb, stt_ = st["xb"], st["hb"], st["stats"]
        bx = tt % 2
        off = 8 * bx
        row = half * HALF + tt * 128
        k.wait("y", st["xb_free"][bx])
        lw = k.dma("y", xb[bx][:], src_d.ap()[row:row + 128, :], ld_x[bx])
        k.wait("s", lw)
        k.wait("s", st["hb_free"][bx])
        st["a1"] = k.op("s", lambda e, bx=bx, off=off: e.activation(
            hb[bx][:], xb[bx][:], AF.Square, accum_out=stt_[:, off:off + 1]))

    def ln_front_b(st, src_d, half, tt):
        xb, hb, stt_ = st["xb"], st["hb"], st["stats"]
        bx = tt % 2
        off = 8 * bx
        a1 = st["a1"]
        k.wait("v", a1)
        k.op("v", lambda e, bx=bx, off=off: e.reduce_sum(stt_[:, off + 1:off + 2],
                                                         xb[bx][:], axis=AX.X))
        k.op("v", lambda e, off=off: e.tensor_scalar_mul(stt_[:, off + 2:off + 3],
                                                         stt_[:, off + 1:off + 2], 1.0 / D))
        k.op("v", lambda e, off=off: e.tensor_scalar_mul(stt_[:, off + 3:off + 4],
                                                         stt_[:, off:off + 1], 1.0 / D))
        k.op("v", lambda e, off=off: e.tensor_mul(stt_[:, off + 4:off + 5],
                                                  stt_[:, off + 2:off + 3],
                                                  stt_[:, off + 2:off + 3]))
        d1 = k.op("v", lambda e, off=off: e.tensor_sub(stt_[:, off + 3:off + 4],
                                                       stt_[:, off + 3:off + 4],
                                                       stt_[:, off + 4:off + 5]))
        k.wait("s", d1)
        k.op("s", lambda e, off=off: e.activation(stt_[:, off + 5:off + 6],
                                                  stt_[:, off + 3:off + 4], AF.Ln,
                                                  bias=epsc[:, 0:1]))
        a2 = k.op("s", lambda e, off=off: e.activation(stt_[:, off + 6:off + 7],
                                                       stt_[:, off + 5:off + 6], AF.Exp,
                                                       scale=-0.5))
        k.wait("v", a2)
        k.op("v", lambda e, off=off: e.tensor_mul(stt_[:, off + 7:off + 8],
                                                  stt_[:, off + 2:off + 3],
                                                  stt_[:, off + 6:off + 7]))
        d2 = k.op("v", lambda e, off=off: e.tensor_scalar_mul(stt_[:, off + 7:off + 8],
                                                              stt_[:, off + 7:off + 8],
                                                              -1.0))
        k.wait("s", d2)
        a3 = k.op("s", lambda e, bx=bx, off=off: e.activation(
            hb[bx][:], xb[bx][:], AF.Identity,
            bias=stt_[:, off + 7:off + 8], scale=stt_[:, off + 6:off + 7]))
        st["xb_free"][bx] = a3
        st["a3"][bx] = a3

    def ln_back(st, half, tt, h1T, dst_dram):
        hb, h2st, ps_tr = st["hb"], st["h2st"], st["ps_tr"]
        bx = tt % 2
        k.wait("t", st["a3"][bx])
        for c in range(NCH):
            pb = c % 4
            k.wait("t", st["ps_tr_free"][pb])
            tv = k.op("t", lambda e, pb=pb, bx=bx, c=c: e.matmul(
                ps_tr[pb][:], hb[bx].ap()[:, c * 128:(c + 1) * 128], ident16[:],
                start=True, stop=True), tr)
            eng = "v" if c % 4 < 3 else "s"
            k.wait(eng, tv)
            if h1T is not None:
                dst = h1T.ap()[:, c * HALF + tt * 128: c * HALF + tt * 128 + 128]
            else:
                k.wait(eng, st["h2st_free"])
                dst = h2st.ap()[:, c * 128:(c + 1) * 128]
            if eng == "v":
                ctok = k.op("v", lambda e, dst=dst, pb=pb: e.tensor_copy(dst,
                                                                         ps_tr[pb][:]))
            else:
                ctok = k.op("s", lambda e, dst=dst, pb=pb: e.copy(dst, ps_tr[pb][:]))
            st["ps_tr_free"][pb] = ctok
            st["cp_last"][eng] = ctok
        st["hb_free"][bx] = (tr, tr.n)
        if h1T is None:
            k.wait("g", st["cp_last"]["v"])
            k.wait("g", st["cp_last"]["s"])
            col = half * HALF + tt * 128
            dstap = dst_dram.ap().rearrange("(c p) m -> p c m", p=128)
            srcap = h2st.ap().rearrange("p (c n) -> p c n", n=128)
            htok = None
            for q4 in range(4):
                htok = k.dma("g", dstap[:, q4 * 8:(q4 + 1) * 8, col:col + 128],
                             srcap[:, q4 * 8:(q4 + 1) * 8, :], st_h)
            st["h2st_free"] = htok
        st["ln_i"] += 1

    def ln_pass(st, src_d, half, h1T=None, dst_dram=None):
        for tt in range(9):
            if tt < 8:
                ln_front_a(st, src_d, half, tt)
            if tt >= 1:
                ln_back(st, half, tt - 1, h1T, dst_dram)
            if tt < 8:
                ln_front_b(st, src_d, half, tt)

    def ffn_half(st, xin_d, xout_d, half, wsrc, first_use_wait=None):
        wres, xsl, x1st, ps_f, h1T = st["wres"], st["xsl"], st["x1st"], st["ps_f"], st["h1T"]
        if first_use_wait is not None:
            k.wait("y", first_use_wait)
        k.waitv("t", sv, sv.n)
        k.waitv("t", ss, ss.n)
        for co in range(8):
            b = co % 2
            k.wait("y", st["wres_free"][b])
            srcap = wsrc.ap().rearrange("(j p) n -> p j n", p=128)
            dstap = wres[b].ap().rearrange("p (j n) -> p j n", n=512)
            lw = None
            for q4 in range(4):
                lw = k.dma("y", dstap[:, q4 * 8:(q4 + 1) * 8, :],
                           srcap[:, q4 * 8:(q4 + 1) * 8, co * 512:(co + 1) * 512], ld_wr[b])
            k.wait("t", lw)
            for tt in range(8):
                gi = st["ffn_i"] * 8 + tt
                pb = gi % 3
                k.wait("t", st["psf_free"][pb])
                tmf = None
                for j in range(NCH):
                    tmf = k.op("t", lambda e, pb=pb, b=b, j=j, tt=tt: e.matmul(
                        ps_f[pb][:],
                        h1T.ap()[:, j * HALF + tt * 128: j * HALF + tt * 128 + 128],
                        wres[b].ap()[:, j * 512:(j + 1) * 512],
                        start=(j == 0), stop=(j == NCH - 1)),
                        mm_f if j == NCH - 1 else None)
                bx = gi % 2
                k.wait("y", st["xsl_free"][bx])
                row = half * HALF + tt * 128
                lx = k.dma("y", xsl[bx][:],
                           xin_d.ap()[row:row + 128, co * 512:(co + 1) * 512], ld_xs[bx])
                k.wait("v", tmf)
                k.wait("v", lx)
                k.wait("v", st["x1st_free"][bx])
                ev = k.op("v", lambda e, pb=pb, bx=bx: e.scalar_tensor_tensor(
                    x1st[bx][:], ps_f[pb][:], 0.5, xsl[bx][:], ALU.mult, ALU.add))
                st["psf_free"][pb] = ev
                st["xsl_free"][bx] = ev
                k.wait("g", ev)
                st["x1st_free"][bx] = k.dma(
                    "g", xout_d.ap()[row:row + 128, co * 512:(co + 1) * 512],
                    x1st[bx][:], st_x1[bx])
            st["wres_free"][b] = (mm_f, mm_f.n)
            st["ffn_i"] += 1

    def ffn_phase(xin_d, xout_d, ln_out_dram, wsrc, first_use_wait, tag):
        with ExitStack() as px:
            sbal = lambda n_, s_, d_=F32: px.enter_context(nc.sbuf_tensor(tag + n_, s_, d_))
            st = {
                "xb": [sbal("xb0", [128, D]), sbal("xb1", [128, D])],
                "hb": [sbal("hb0", [128, D], BF16), sbal("hb1", [128, D], BF16)],
                "stats": sbal("stats", [128, 16]),
                "h2st": sbal("h2st", [128, NCH * 128], BF16),
                "h1T": sbal("h1T", [128, NCH * HALF], BF16),
                "wres": [sbal("wres0", [128, NCH * 512], BF16),
                         sbal("wres1", [128, NCH * 512], BF16)],
                "xsl": [sbal("xsl0", [128, 512]), sbal("xsl1", [128, 512])],
                "x1st": [sbal("x1st0", [128, 512]), sbal("x1st1", [128, 512])],
                "ps_tr": [px.enter_context(nc.psum_tensor(tag + "ps_tr%d" % i,
                                                           [128, 128], F32))
                          for i in range(4)],
                "ps_f": [px.enter_context(nc.psum_tensor(tag + "ps_f%d" % i,
                                                          [128, 512], F32))
                         for i in range(3)],
                "ps_warm": px.enter_context(nc.psum_tensor(tag + "ps_warm", [16, 16], F32)),
                "ln_i": 0, "ffn_i": 0,
                "xb_free": [None, None], "hb_free": [None, None],
                "psf_free": [None, None, None], "xsl_free": [None, None],
                "x1st_free": [None, None], "wres_free": [None, None],
                "ps_tr_free": [None, None, None, None], "h2st_free": None,
                "a3": [None, None],
                "cp_last": {"v": None, "s": None},
            }
            warm_ctx["ps"] = st["ps_warm"]
            for half in (0, 1):
                ln_pass(st, xin_d, half, h1T=st["h1T"])
                ffn_half(st, xin_d, xout_d, half, wsrc,
                         first_use_wait=first_use_wait if st["ffn_i"] == 0 else None)
                if ln_out_dram is not None:
                    k.waitv("y", st_x1[0], st_x1[0].n)
                    k.waitv("y", st_x1[1], st_x1[1].n)
                    ln_pass(st, xout_d, half, h1T=None, dst_dram=ln_out_dram)
            k.barrier()

    ffn_phase(x_in, x1_d, h2t_d, wf16, first_use_wait=(cc_wf, 1), tag="p2_")

    ld_w2 = [k.ch("ld_w2%d" % i) for i in range(6)]
    mm_k = k.ch("mm_k")
    st_kv = [k.ch("st_kv0"), k.ch("st_kv1")]
    mm_n = k.ch("mm_n")
    ld_n = [k.ch("ld_n0"), k.ch("ld_n1")]
    ld_kn = k.ch("ld_kn")

    with ExitStack() as p3:
        sbal = lambda n_, s_, d_=F32: p3.enter_context(nc.sbuf_tensor(n_, s_, d_))
        h2res = p3.enter_context(nc.sbuf_tensor("h2res", [128, NCH * HALF], BF16))
        wslab = [p3.enter_context(nc.sbuf_tensor("wslab%d" % i, [128, NCH * 256], BF16))
                 for i in range(6)]
        kst = [p3.enter_context(nc.sbuf_tensor("kst0", [128, 512], BF16)),
               p3.enter_context(nc.sbuf_tensor("kst1", [128, 512], BF16))]
        sqs = [p3.enter_context(nc.sbuf_tensor("sqs0", [128, 512], BF16)),
               p3.enter_context(nc.sbuf_tensor("sqs1", [128, 512], BF16))]
        ntmp = sbal("ntmp", [1, 8])
        nrm = sbal("nrm", [16, 8])
        qmaxr = sbal("qmaxr", [1, NH])
        kmaxr = sbal("kmaxr", [1, NH])
        knsb = sbal("knsb", [16, 4])
        nb1 = sbal("nb1", [1, NH])
        ps_kv = [p3.enter_context(nc.psum_tensor("ps_kv%d" % i, [128, 512], F32))
                 for i in range(3)]
        ps_n = [p3.enter_context(nc.psum_tensor("ps_n%d" % i, [1, 512], F32))
                for i in range(4)]
        ps_x = p3.enter_context(nc.psum_tensor("ps_x", [128, 64], F32))
        warm_ctx["ps"] = None

        st3 = {"wslab_free": [None] * 6, "pskv_free": [None, None, None],
               "kst_free": [None, None], "wi": 0, "pre": {}, "pv": 0,
               "psn_free": [None, None, None, None], "nrm_last": None,
               "pskv3": True,
               "sq_free": [None, None]}

        def load_h2res(half):
            srcap = h2t_d.ap().rearrange("(j p) m -> p j m", p=128)
            dstap = h2res.ap().rearrange("p (j m) -> p j m", m=HALF)
            lw = None
            for q4 in range(4):
                lw = k.dma("y", dstap[:, q4 * 8:(q4 + 1) * 8, :],
                           srcap[:, q4 * 8:(q4 + 1) * 8, half * HALF:(half + 1) * HALF],
                           ld_h2)
            return lw

        def slab_load_kq(wsrc, c):
            b = st3["wi"] % 6
            st3["wi"] += 1
            srcw = wsrc.ap().rearrange("(j p) n -> p j n", p=128)
            k.wait("y", st3["wslab_free"][b])
            dstw = wslab[b].ap().rearrange("p (j n) -> p j n", n=256)[:, :, 0:128]
            lw = None
            for q4 in range(4):
                lw = k.dma("y", dstw[:, q4 * 8:(q4 + 1) * 8, :],
                           srcw[:, q4 * 8:(q4 + 1) * 8, c * 128:(c + 1) * 128], ld_w2[b])
            return b, lw

        def slab_load_v(h):
            b = st3["wi"] % 6
            st3["wi"] += 1
            srcw = wvt.ap().rearrange("(j p) n -> p j n", p=128)
            k.wait("y", st3["wslab_free"][b])
            dstw = wslab[b].ap().rearrange("p (j n) -> p j n", n=256)
            lw = None
            for q4 in range(4):
                lw = k.dma("y", dstw[:, q4 * 8:(q4 + 1) * 8, :],
                           srcw[:, q4 * 8:(q4 + 1) * 8, h * 256:(h + 1) * 256], ld_w2[b])
            return b, lw

        def kq_pass(half, wsrc, bias_sb, dst_dram, h2w, maxrow, split):
            for c in range(NCH):
                if ("kq", half, c) in st3["pre"]:
                    b, lw = st3["pre"].pop(("kq", half, c))
                else:
                    b, lw = slab_load_kq(wsrc, c)
                k.wait("t", lw)
                k.wait("t", h2w)
                sq_toks = [None, None]
                for t2 in range(2):
                    pb = st3["pv"] % 3
                    st3["pv"] += 1
                    k.wait("t", st3["pskv_free"][pb])
                    tmk = None
                    for j in range(NCH):
                        tmk = k.op("t", lambda e, pb=pb, b=b, j=j, t2=t2: e.matmul(
                            ps_kv[pb][:], wslab[b].ap()[:, j * 256:j * 256 + 128],
                            h2res.ap()[:, j * HALF + t2 * 512: j * HALF + t2 * 512 + 512],
                            start=(j == 0), stop=(j == NCH - 1)),
                            mm_k if j == NCH - 1 else None)
                    gb = t2
                    k.wait("s", tmk)
                    k.wait("s", st3["kst_free"][gb])
                    ev = k.op("s", lambda e, pb=pb, c=c, gb=gb, bias_sb=bias_sb:
                              e.activation(kst[gb][:], ps_kv[pb][:], AF.Identity,
                                           bias=bias_sb.ap()[:, c:c + 1]))
                    st3["pskv_free"][pb] = ev
                    k.wait("g", ev)
                    if split:
                        ro = half * D + c * 128
                        col = t2 * 512
                    else:
                        ro = c * 128
                        col = half * HALF + t2 * 512
                    st3["kst_free"][gb] = k.dma(
                        "g", dst_dram.ap()[ro:ro + 128, col:col + 512],
                        kst[gb][:], st_kv[gb])
                    k.wait("s", st3["sq_free"][gb])
                    sq_toks[t2] = k.op("s", lambda e, gb=gb: e.activation(
                        sqs[gb][:], kst[gb][:], AF.Square))
                for t2 in range(2):
                    g_ = 2 * half + t2
                    k.wait("t", sq_toks[t2])
                    if c % 2 == 0:
                        k.wait("t", st3["psn_free"][g_])
                    tn = k.op("t", lambda e, g_=g_, t2=t2, c=c: e.matmul(
                        ps_n[g_][:], onescol_bf[:], sqs[t2].ap()[:, 0:512],
                        start=(c % 2 == 0), stop=(c % 2 == 1)), mm_k)
                    st3["sq_free"][t2] = (mm_k, mm_k.n)
                    if c % 2 == 1:
                        h = c // 2
                        k.wait("v", tn)
                        k.op("v", lambda e, g_=g_: e.reduce_max(ntmp.ap()[0:1, g_:g_ + 1],
                                                                ps_n[g_][:], axis=AX.X))
                        vt = k.op("v", lambda e, g_=g_, h=h, maxrow=maxrow: e.tensor_max(
                            maxrow.ap()[0:1, h:h + 1], maxrow.ap()[0:1, h:h + 1],
                            ntmp.ap()[0:1, g_:g_ + 1]))
                        st3["psn_free"][g_] = vt
                        st3["nrm_last"] = vt
                st3["wslab_free"][b] = (mm_k, mm_k.n)

        def v_pass(half, h2w):
            k.waitv("v", ss, ss.n)
            for h in range(NH):
                if ("v", half, h) in st3["pre"]:
                    b, lw = st3["pre"].pop(("v", half, h))
                else:
                    b, lw = slab_load_v(h)
                k.wait("t", lw)
                k.wait("t", h2w)
                for tt in range(8):
                    pb = st3["pv"] % 3
                    st3["pv"] += 1
                    k.wait("t", st3["pskv_free"][pb])
                    for j in range(NCH):
                        k.op("t", lambda e, pb=pb, b=b, j=j, tt=tt: e.matmul(
                            ps_kv[pb][:, 0:256],
                            h2res.ap()[:, j * HALF + tt * 128: j * HALF + tt * 128 + 128],
                            wslab[b].ap()[:, j * 256:(j + 1) * 256],
                            start=(j == 0), stop=False))
                    tmk = k.op("t", lambda e, pb=pb, h=h: e.matmul(
                        ps_kv[pb][:, 0:256], ones_bf[:], bv16.ap()[:, h * 256:(h + 1) * 256],
                        start=False, stop=True), mm_k)
                    gb = tt % 2
                    k.wait("v", tmk)
                    k.wait("v", st3["kst_free"][gb])
                    ev = k.op("v", lambda e, pb=pb, gb=gb: e.tensor_copy(
                        kst[gb][:, 0:256], ps_kv[pb][:, 0:256]))
                    st3["pskv_free"][pb] = ev
                    k.wait("g", ev)
                    row = half * NH * HALF + h * HALF + tt * 128
                    st3["kst_free"][gb] = k.dma("g", v_loc.ap()[row:row + 128, :],
                                                kst[gb][:, 0:256], st_kv[gb])
                st3["wslab_free"][b] = (mm_k, mm_k.n)

        mz = k.op("v", lambda e: e.memset(kmaxr[:], 0.0))
        mz = k.op("v", lambda e: e.memset(qmaxr[:], 0.0))
        for half in (0, 1):
            if half == 1:
                k.waitv("y", mm_k, mm_k.n)
            h2w = load_h2res(half)
            if half == 0:
                k.waitv("y", cc_wk, 1)
                k.waitv("y", cc_wv, 1)
            kq_pass(half, wkt, bias_k, k_loc, h2w, kmaxr, split=True)
            k.waitv("g", st_kv[0], st_kv[0].n)
            k.waitv("g", st_kv[1], st_kv[1].n)
            k.op("g", lambda e, half=half: e.collective_compute(
                "AllGather", ALU.bypass, replica_groups=R4,
                ins=[k_loc.ap()[half * D:(half + 1) * D, :].opt()],
                outs=[k_full[half].ap().opt()]), cc_k[half])
        ktok = st3["nrm_last"]
        k.wait("g", ktok)
        kstn = k.dma("g", kn_loc.ap(), kmaxr[:], st_n)
        k.wait("g", kstn)
        k.op("g", lambda e: e.collective_compute(
            "AllGather", ALU.bypass, replica_groups=R4,
            ins=[kn_loc.ap().opt()], outs=[kn_g.ap().opt()]), cc_n)

        for half in (0, 1):
            if half == 0:
                for h in range(3):
                    st3["pre"][("v", 0, h)] = slab_load_v(h)
            k.waitv("y", mm_k, mm_k.n)
            h2w = load_h2res(half)
            v_pass(half, h2w)
            k.waitv("g", st_kv[0], st_kv[0].n)
            k.waitv("g", st_kv[1], st_kv[1].n)
            k.op("g", lambda e, half=half: e.collective_compute(
                "AllGather", ALU.bypass, replica_groups=R4,
                ins=[v_loc.ap()[half * NH * HALF:(half + 1) * NH * HALF, :].opt()],
                outs=[v_full[half].ap().opt()]), cc_v[half])

        for half in (0, 1):
            if half == 0:
                k.waitv("y", cc_wq, 1)
                for c in range(3):
                    st3["pre"][("kq", 0, c)] = slab_load_kq(wqt, c)
            k.waitv("y", mm_k, mm_k.n)
            h2w = load_h2res(half)
            kq_pass(half, wqt, bias_q, q_loc, h2w, qmaxr, split=False)
        qtok = st3["nrm_last"]

        k.waitv("y", cc_n, 1)
        kln = None
        for r in range(4):
            kln = k.dma("y", knsb[:, r:r + 1],
                        kn_g.ap()[r:r + 1, :].rearrange("one h -> h one"), ld_kn)

        k.wait("v", kln)
        k.op("v", lambda e: e.reduce_max(nrm.ap()[0:16, 4:5], knsb[:], axis=AX.X))
        k.wait("t", qtok)
        tq1 = k.op("t", lambda e: e.transpose(ps_x[0:16, 0:1], qmaxr[:],
                                              identf.ap()[0:1, 0:1]), mm)
        k.wait("v", tq1)
        k.op("v", lambda e: e.tensor_copy(nrm.ap()[0:16, 5:6], ps_x[0:16, 0:1]))
        d4 = k.op("v", lambda e: e.tensor_mul(nrm.ap()[0:16, 6:7], nrm.ap()[0:16, 4:5],
                                              nrm.ap()[0:16, 5:6]))
        k.wait("s", d4)
        a4 = k.op("s", lambda e: e.activation(nrm.ap()[0:16, 7:8], nrm.ap()[0:16, 6:7],
                                              AF.Sqrt))
        k.wait("v", a4)
        d5 = k.op("v", lambda e: e.tensor_scalar_mul(nrm.ap()[0:16, 7:8],
                                                     nrm.ap()[0:16, 7:8], -1.0))
        k.wait("t", d5)
        tq2 = k.op("t", lambda e: e.transpose(ps_x[0:1, 0:16], nrm.ap()[0:16, 7:8],
                                              identf.ap()[0:16, 0:16]), mm)
        k.wait("v", tq2)
        d6 = k.op("v", lambda e: e.tensor_copy(nb1[:], ps_x[0:1, 0:16]))
        k.wait("t", d6)
        tq3 = k.op("t", lambda e: e.matmul(ps_x[:, 0:16], ones_f[:], nb1[:],
                                           start=True, stop=True), mm)
        k.wait("v", tq3)
        k.op("v", lambda e: e.tensor_copy(expb[:], ps_x[:, 0:16]))
    k.barrier()

    ld_k4 = [k.ch("ld_k40"), k.ch("ld_k41")]
    ld_v4 = [k.ch("ld_v40"), k.ch("ld_v41")]
    ld_x1 = [k.ch("ld_x1a"), k.ch("ld_x1c")]
    st_x2 = [k.ch("st_x20"), k.ch("st_x21")]
    tl = k.ch("tl")
    av = k.ch("av")

    with ExitStack() as p4:
        kh = [p4.enter_context(nc.sbuf_tensor("kh0", [128, 2 * LK], BF16)),
              p4.enter_context(nc.sbuf_tensor("kh1", [128, 2 * LK], BF16))]
        vh = [p4.enter_context(nc.sbuf_tensor("vh0", [128, MCH * 257], BF16)),
              p4.enter_context(nc.sbuf_tensor("vh1", [128, MCH * 257], BF16))]
        qh = [p4.enter_context(nc.sbuf_tensor("qh0", [128, 2 * TOK], BF16)),
              p4.enter_context(nc.sbuf_tensor("qh1", [128, 2 * TOK], BF16))]
        p_sb = [p4.enter_context(nc.sbuf_tensor("p0", [128, 1024], BF16)),
                p4.enter_context(nc.sbuf_tensor("p1", [128, 1024], BF16))]
        x1sl = [p4.enter_context(nc.sbuf_tensor("x1sl0", [128, 256], F32)),
                p4.enter_context(nc.sbuf_tensor("x1sl1", [128, 256], F32))]
        x2st = [p4.enter_context(nc.sbuf_tensor("x2st0", [128, 256], F32)),
                p4.enter_context(nc.sbuf_tensor("x2st1", [128, 256], F32))]
        rcp = p4.enter_context(nc.sbuf_tensor("rcp", [128, 1], F32))
        ps_l = [p4.enter_context(nc.psum_tensor("ps_l0", [128, 1024], F32)),
                p4.enter_context(nc.psum_tensor("ps_l1", [128, 1024], F32))]
        ps_av = [p4.enter_context(nc.psum_tensor("ps_av%d" % i, [128, 512], F32))
                 for i in range(4)]

        k.op("v", lambda e: e.memset(
            vh[0].ap().rearrange("p (c n) -> p c n", n=257)[:, :, 256:257], 1.0))
        vones = k.op("v", lambda e: e.memset(
            vh[1].ap().rearrange("p (c n) -> p c n", n=257)[:, :, 256:257], 1.0))

        def load_head(h, hb):
            ktok_ = None
            for fc in range(2):
                for r in range(4):
                    for hf in range(2):
                        ktok_ = k.dma(
                            "y",
                            kh[hb][:, fc * LK + r * TOK + hf * HALF:
                                   fc * LK + r * TOK + (hf + 1) * HALF],
                            k_full[hf].ap()[r * D + h * 256 + fc * 128:
                                            r * D + h * 256 + fc * 128 + 128, :],
                            ld_k4[hb])
            for fc in range(2):
                ktok_ = k.dma("y", qh[hb][:, fc * TOK:(fc + 1) * TOK],
                              q_loc.ap()[h * 256 + fc * 128: h * 256 + fc * 128 + 128, :],
                              ld_k4[hb])
            vtok_ = None
            for r in range(4):
                for hf in range(2):
                    for s2 in range(2):
                        base = (r * NH + h) * HALF + s2 * 512
                        cbase = r * 16 + hf * 8 + s2 * 4
                        dst = vh[hb].ap().rearrange("p (c n) -> p c n", n=257)[
                            :, cbase:cbase + 4, 0:256]
                        src = v_full[hf].ap()[base:base + 512, :].rearrange(
                            "(c p) n -> p c n", p=128)
                        vtok_ = k.dma("g", dst, src, ld_v4[hb])
            return ktok_, vtok_

        for hf in range(2):
            k.waitv("y", cc_k[hf], 1)
            k.waitv("y", cc_v[hf], 1)
            k.waitv("g", cc_k[hf], 1)
            k.waitv("g", cc_v[hf], 1)
        k.wait("g", vones)
        marks = {0: load_head(0, 0)}
        x1_free = [None, None]
        x2_free = [None, None]
        exp_toks = {}
        ev4_last = None

        mc_glob = 0
        for h in range(NH):
            hb = h % 2
            k.wait("t", marks[h][0])
            k.wait("t", marks[h][1])
            if h == 0:
                k.wait("t", vones)
            if h + 1 < NH:
                if h >= 1:
                    k.waitv("y", av, 128 * h)
                    k.waitv("g", av, 128 * h)
                marks[h + 1] = load_head(h + 1, (h + 1) % 2)
            for lg in range(4):
                r0 = mc_glob // 2
                NR = MCH // 2
                for r in range(NR):
                    ri = r0 + r
                    pb = ri % 2
                    k.wait("t", exp_toks.pop(ri - 2, None))
                    for jj in range(2):
                        j = 2 * r + jj
                        for fc in range(2):
                            k.op("t", lambda e, pb=pb, hb=hb, fc=fc, j=j, jj=jj,
                                 lg=lg: e.matmul(
                                ps_l[pb].ap()[:, jj * 512:(jj + 1) * 512],
                                kh[hb].ap()[:, fc * LK + j * 128: fc * LK + j * 128 + 128],
                                qh[hb].ap()[:, fc * TOK + lg * 512: fc * TOK + lg * 512 + 512],
                                start=(fc == 0), stop=(fc == 1)),
                                tl if (fc == 1 and jj == 1) else None)
                    if r >= 1:
                        k.wait("t", exp_toks.get(ri - 1))
                        if r == 1:
                            k.wait("t", ev4_last)
                        for jj in range(2):
                            jm = 2 * (r - 1) + jj
                            for lc in range(4):
                                k.op("t", lambda e, lc=lc, hb=hb, jm=jm, jj=jj,
                                     pp=(ri - 1) % 2: e.matmul(
                                    ps_av[lc][:, 0:257],
                                    p_sb[pp].ap()[:, jj * 512 + lc * 128: jj * 512 + (lc + 1) * 128],
                                    vh[hb].ap()[:, jm * 257:(jm + 1) * 257],
                                    start=(jm == 0), stop=False),
                                    av if (lc == 3 and jj == 1) else None)
                    k.waitv("s", tl, ri + 1)
                    k.waitv("s", av, max(0, ri - 1))
                    exp_toks[ri] = k.op("s", lambda e, pb=pb, h=h: e.activation(
                        p_sb[pb][:], ps_l[pb][:], AF.Exp, bias=expb.ap()[:, h:h + 1]))
                k.wait("t", exp_toks.get(r0 + NR - 1))
                for jj in range(2):
                    jm = MCH - 2 + jj
                    for lc in range(4):
                        k.op("t", lambda e, lc=lc, hb=hb, jm=jm, jj=jj,
                             pp=(r0 + NR - 1) % 2: e.matmul(
                            ps_av[lc][:, 0:257],
                            p_sb[pp].ap()[:, jj * 512 + lc * 128: jj * 512 + (lc + 1) * 128],
                            vh[hb].ap()[:, jm * 257:(jm + 1) * 257],
                            start=False, stop=(jm == MCH - 1)),
                            av if (lc == 3 and jj == 1) else None)
                mc_glob += MCH
                for lc in range(4):
                    bx = lc % 2
                    row = lg * 512 + lc * 128
                    k.wait("y", x1_free[bx])
                    lx = k.dma("y", x1sl[bx][:],
                               x1_d.ap()[row:row + 128, h * 256:(h + 1) * 256], ld_x1[bx])
                    if lc == 0:
                        k.waitv("v", av, av.n)
                    k.wait("v", lx)
                    k.wait("v", x2_free[bx])
                    k.op("v", lambda e, lc=lc: e.reciprocal(rcp[:], ps_av[lc][:, 256:257]))
                    ev = k.op("v", lambda e, lc=lc, bx=bx: e.scalar_tensor_tensor(
                        x2st[bx][:], ps_av[lc][:, 0:256], rcp[:, 0:1], x1sl[bx][:],
                        ALU.mult, ALU.add))
                    x1_free[bx] = ev
                    ev4_last = ev
                    k.wait("g", ev)
                    x2_free[bx] = k.dma("g", x2_d.ap()[row:row + 128,
                                                       h * 256:(h + 1) * 256],
                                        x2st[bx][:], st_x2[bx])
    k.barrier()

    ffn_phase(x2_d, out_d, None, wf16, first_use_wait=None, tag="p5_")

    k.waitv("g", st_x1[0], st_x1[0].n)
    k.waitv("g", st_x1[1], st_x1[1].n)
    k.waitv("y", st_x1[0], st_x1[0].n)
    k.waitv("y", st_x1[1], st_x1[1].n)

    with nc.Block() as block:
        @block.sync
        def _(e):
            for f in k.rec["y"]:
                f(e)

        @block.tensor
        def _(e):
            for f in k.rec["t"]:
                f(e)

        @block.vector
        def _(e):
            for f in k.rec["v"]:
                f(e)

        @block.scalar
        def _(e):
            for f in k.rec["s"]:
                f(e)

        @block.gpsimd
        def _(e):
            for f in k.rec["g"]:
                f(e)

    top.close()
    return nc


def kernel(L0_attn_x, L0_attn_Wq, L0_attn_bq, L0_attn_Wk, L0_attn_bk,
           L0_attn_Wv, L0_attn_bv, L0_attn_bias_weight, L0_ffn_weight):
    if "nc" not in _CACHE:
        _CACHE["nc"] = build_nc()
    nc = _CACHE["nc"]

    x = np.ascontiguousarray(np.asarray(L0_attn_x, dtype=np.float32)).reshape(16384, D)
    Wq = np.asarray(L0_attn_Wq, dtype=np.float32)
    Wk = np.asarray(L0_attn_Wk, dtype=np.float32)
    Wv = np.asarray(L0_attn_Wv, dtype=np.float32)
    Wf = np.asarray(L0_ffn_weight, dtype=np.float32)
    bq = np.asarray(L0_attn_bq, dtype=np.float32).reshape(1, D)
    bk = np.asarray(L0_attn_bk, dtype=np.float32).reshape(1, D)
    bv = np.asarray(L0_attn_bv, dtype=np.float32).reshape(1, D)
    bw = np.asarray(L0_attn_bias_weight, dtype=np.float32).reshape(1, NH * 3)

    in_maps = []
    for i in range(NCORES):
        sl = slice(i * CIN_SH, (i + 1) * CIN_SH)
        in_maps.append({
            "x": np.ascontiguousarray(x[i * TOK:(i + 1) * TOK]),
            "wq_sl": np.ascontiguousarray(Wq[:, sl]),
            "wk_sl": np.ascontiguousarray(Wk[:, sl]),
            "wv_sl": np.ascontiguousarray(Wv[:, sl]),
            "wf_sl": np.ascontiguousarray(Wf[sl, :]),
            "bq": bq, "bk": bk, "bv": bv, "bw": bw,
        })

    res = run_bass_kernel_spmd(nc, in_maps, core_ids=list(range(NCORES)))
    out = np.concatenate([r["out"] for r in res.results], axis=0)
    return out.reshape(2, 8192, D)


# version 16
# speedup vs baseline: 1.0273x; 1.0273x over previous
import numpy as np
import ml_dtypes
from contextlib import ExitStack

import concourse.bass as bass
import concourse.mybir as mybir
from concourse.bass_utils import run_bass_kernel_spmd

F32 = mybir.dt.float32
BF16 = mybir.dt.bfloat16
AF = mybir.ActivationFunctionType
ALU = mybir.AluOpType
AX = mybir.AxisListType

NCORES = 8
TOK = 2048
D = 4096
NCH = 32
NH = 16
LK = 8192
MCH = LK // 128
HALF = 1024
CIN_SH = D // NCORES
EPS = 1e-5
SCALE = 1.0 / np.sqrt(256.0)

R8 = [[0, 1, 2, 3, 4, 5, 6, 7]]
R4 = [[0, 1, 2, 3], [4, 5, 6, 7]]

_CACHE = {}


def _build_tables():
    g = [1.0, 1.0, 1.0]
    dim, n = 3, 8
    masks = sorted(range(n), key=lambda m: (bin(m).count("1"), m))
    inv = {m: i for i, m in enumerate(masks)}
    C = np.zeros((n, n, n), dtype=np.float32)
    for i, a in enumerate(masks):
        for j, b in enumerate(masks):
            sw, at = 0, a >> 1
            while at:
                sw += bin(at & b).count("1")
                at >>= 1
            s = -1.0 if sw & 1 else 1.0
            for t in range(dim):
                if a >> t & 1 and b >> t & 1:
                    s *= float(g[t])
            C[i, j, inv[a ^ b]] = s
    grades = np.array([bin(m).count("1") for m in masks])
    rev = np.where(grades * (grades - 1) // 2 % 2 == 1, -1.0, 1.0).astype(np.float32)
    met = np.array([float(np.prod([g[t] for t in range(dim) if m >> t & 1] + [1.0]))
                    for m in masks], dtype=np.float32)
    return C, rev, met


def _g_const():
    C, rev, met = _build_tables()
    bivk = C[:, :, [4, 5, 6]]
    G = np.zeros((4, 128, 128), dtype=np.float32)
    for d in range(16):
        for i in range(8):
            for j in range(8):
                for c in range(3):
                    G[c, d * 8 + i, d * 8 + j] = rev[i] * bivk[i, j, c]
            G[3, d * 8 + i, d * 8 + i] = SCALE * rev[i] * met[i]
    return G


class Ch:
    __slots__ = ("sem", "n", "name")

    def __init__(self, sem, name):
        self.sem = sem
        self.n = 0
        self.name = name

    def bump(self, k=1):
        self.n += k
        return self.n


class K:

    def __init__(self, nc):
        self.nc = nc
        self.rec = {e: [] for e in ("y", "t", "v", "s", "g")}
        self._chs = {}
        self.owner = {}
        self._burst = {e: None for e in ("y", "t", "v", "s", "g")}
        self.selfch = {"v": self.ch("sv"), "s": self.ch("ss")}

    def ch(self, name):
        if name not in self._chs:
            self._chs[name] = Ch(self.nc.alloc_semaphore(name), name)
        return self._chs[name]

    def _raw(self, eng, fn):
        self.rec[eng].append(fn)

    def on(self, eng, fn):
        self._burst[eng] = None
        self.rec[eng].append(fn)

    def wait(self, eng, tok):
        if tok is None:
            return
        def f(e, t=tok):
            v = int(t[1])
            if v > 0:
                e.wait_ge(t[0].sem, v)
        self.on(eng, f)

    def waitv(self, eng, ch, val):
        b = None
        for e_, bb in self._burst.items():
            if bb is not None and bb[0] is ch:
                b = bb
                break
        if b is not None and int(val) == ch.n:
            self.wait(eng, b[1][-1])
        else:
            self.wait(eng, [ch, val])

    def op(self, eng, fn, ch=None, k=1):
        if eng in ("v", "s"):
            assert ch is None
            sc = self.selfch[eng]
            if sc.n > 0:
                def w(e, s=sc.sem, v=sc.n):
                    e.wait_ge(s, v)
                self.on(eng, w)
            else:
                self._burst[eng] = None
            v2 = sc.bump(1)
            self._raw(eng, lambda e, f=fn, s=sc.sem: f(e).then_inc(s, 1))
            return [sc, v2]
        if ch is None:
            self.on(eng, fn)
            return None
        self.owner.setdefault(ch.name, eng)
        self._burst[eng] = None
        v2 = ch.bump(k)
        self._raw(eng, lambda e, f=fn, s=ch.sem, k=k: f(e).then_inc(s, k))
        return [ch, v2]

    def dma(self, eng, out, in_, ch, **kw):
        self.owner.setdefault(ch.name, eng)
        tok = [ch, ch.bump(16)]
        b = self._burst[eng]
        if b is not None and b[0] is ch:
            b[1].append(tok)
            for t in b[1]:
                t[1] = ch.n
        else:
            self._burst[eng] = (ch, [tok])
        self._raw(eng, lambda e, o=out, i=in_, kw=kw, s=ch.sem:
                  e.dma_start(out=o, in_=i, **kw).then_inc(s, 16))
        return tok

    def barrier(self):
        ph = self.ch("ph")
        drains = {e: [] for e in ("y", "t", "v", "s", "g")}
        for name, eng in self.owner.items():
            if name.startswith("cc_"):
                continue
            c = self._chs[name]
            if c.n > 0:
                drains[eng].append((c.sem, c.n))
        for e_ in ("v", "s"):
            sc = self.selfch[e_]
            if sc.n > 0:
                drains[e_].append((sc.sem, sc.n))
        tgt = ph.n + 5
        ph.n = tgt
        for eng in ("y", "t", "v", "s", "g"):
            def f(e, ds=tuple(drains[eng]), s=ph.sem, t=tgt):
                for sem_, v_ in ds:
                    e.wait_ge(sem_, v_)
                e.nop().then_inc(s, 1)
                e.wait_ge(s, t)
            self.on(eng, f)


def build_nc():
    nc = bass.Bass("TRN2", target_bir_lowering=False, debug=False)
    k = K(nc)
    sv = k.selfch["v"]
    ss = k.selfch["s"]
    warm_ctx = {"ps": None, "ones": None}

    def warm():
        ps = warm_ctx["ps"]
        if ps is None:
            return
        on = warm_ctx["ones"]
        k.op("t", lambda e, ps=ps, on=on: e.matmul(ps[0:16, 0:16], on[0:1, 0:16],
                                                   on[0:1, 0:16], start=True, stop=True))

    x_in = nc.dram_tensor("x", [TOK, D], F32, kind="ExternalInput")
    wq_sl = nc.dram_tensor("wq_sl", [D, CIN_SH], F32, kind="ExternalInput")
    wk_sl = nc.dram_tensor("wk_sl", [D, CIN_SH], F32, kind="ExternalInput")
    wv_sl = nc.dram_tensor("wv_sl", [D, CIN_SH], F32, kind="ExternalInput")
    wf_sl = nc.dram_tensor("wf_sl", [CIN_SH, D], F32, kind="ExternalInput")
    bq_in = nc.dram_tensor("bq", [1, D], F32, kind="ExternalInput")
    bk_in = nc.dram_tensor("bk", [1, D], F32, kind="ExternalInput")
    bv_in = nc.dram_tensor("bv", [1, D], F32, kind="ExternalInput")
    bw_in = nc.dram_tensor("bw", [1, NH * 3], F32, kind="ExternalInput")
    out_d = nc.dram_tensor("out", [TOK, D], F32, kind="ExternalOutput")

    wqt_loc = nc.dram_tensor("wqt_loc", [CIN_SH, D], BF16)
    wkt_loc = nc.dram_tensor("wkt_loc", [CIN_SH, D], BF16)
    wvt_loc = nc.dram_tensor("wvt_loc", [CIN_SH, D], BF16)
    wf_loc = nc.dram_tensor("wf_loc", [CIN_SH, D], BF16)
    wqt = nc.dram_tensor("wqt", [D, D], BF16, addr_space="Shared")
    wkt = nc.dram_tensor("wkt", [D, D], BF16, addr_space="Shared")
    wvt = nc.dram_tensor("wvt", [D, D], BF16, addr_space="Shared")
    wf16 = nc.dram_tensor("wf16", [D, D], BF16, addr_space="Shared")
    h2t_d = nc.dram_tensor("h2t", [D, TOK], BF16)
    x1_d = nc.dram_tensor("x1", [TOK, D], F32)
    x2_d = nc.dram_tensor("x2", [TOK, D], F32)
    q_loc = nc.dram_tensor("q_loc", [D, TOK], BF16)
    k_loc = nc.dram_tensor("k_loc", [2 * D, HALF], BF16)
    v_loc = nc.dram_tensor("v_loc", [2 * NH * HALF, 256], BF16)
    k_full = [nc.dram_tensor("k_full%d" % i, [4 * D, HALF], BF16) for i in range(2)]
    v_full = [nc.dram_tensor("v_full%d" % i, [4 * NH * HALF, 256], BF16) for i in range(2)]
    kn_loc = nc.dram_tensor("kn_loc", [1, NH], F32)
    kn_g = nc.dram_tensor("kn_g", [4, NH], F32)

    ident16_h = nc.inline_tensor(np.eye(128, dtype=ml_dtypes.bfloat16), name="ident16")
    identf_h = nc.inline_tensor(np.eye(128, dtype=np.float32), name="identf")
    g_h = nc.inline_tensor(_g_const(), name="gconst")

    top = ExitStack()
    SBt = lambda n_, s_, d_=F32: top.enter_context(nc.sbuf_tensor(n_, s_, d_))
    ident16 = SBt("ident16s", [128, 128], BF16)
    identf = SBt("identfs", [128, 128], F32)
    ones_f = SBt("ones_f", [1, 128], F32)
    ones_bf = SBt("ones_bf", [1, 128], BF16)
    onescol_bf = SBt("onescol_bf", [128, 1], BF16)
    bias_q = SBt("bias_q", [128, NCH], F32)
    bias_k = SBt("bias_k", [128, NCH], F32)
    bv16 = SBt("bv16", [1, D], BF16)
    expb = SBt("expb", [128, NH], F32)
    epsc = SBt("epsc", [128, 1], F32)
    warm_ctx["ones"] = ones_bf

    ld_misc = k.ch("ld_misc")
    st_n = k.ch("st_n")
    mm = k.ch("mm")
    cc_wq = k.ch("cc_wq")
    cc_wk = k.ch("cc_wk")
    cc_wv = k.ch("cc_wv")
    cc_wf = k.ch("cc_wf")
    cc_k = [k.ch("cc_k0"), k.ch("cc_k1")]
    cc_v = [k.ch("cc_v0"), k.ch("cc_v1")]
    cc_n = k.ch("cc_n")

    with ExitStack() as p0:
        sbal = lambda n_, s_, d_=F32: p0.enter_context(nc.sbuf_tensor(n_, s_, d_))
        psal = lambda n_, s_, d_=F32: p0.enter_context(nc.psum_tensor(n_, s_, d_))
        g_sb = sbal("g_sb", [128, 4 * 128])
        m_sb = sbal("m_sb", [128, NH * 128])
        tmp_sb = sbal("tmp_sb", [128, 128])
        bwb = sbal("bwb", [128, 48])
        bqraw = sbal("bqraw", [128, NCH])
        bvrow = sbal("bvrow", [1, D])
        bwrow = sbal("bwrow", [1, 48])
        ps_bw = psal("ps_bw", [128, 64])
        ps_b = [psal("ps_b0", [128, 64]), psal("ps_b1", [128, 64])]

        k.dma("y", ident16[:], ident16_h.ap(), ld_misc)
        k.dma("y", identf[:], identf_h.ap(), ld_misc)
        k.dma("y", g_sb.ap().rearrange("p (c n) -> p c n", n=128),
              g_h.ap().rearrange("c p n -> p c n"), ld_misc)
        k.dma("y", bwrow[:], bw_in.ap(), ld_misc)
        k.dma("y", bvrow[:], bv_in.ap(), ld_misc)
        bqrow = sbal("bqrow", [1, D])
        bkrow = sbal("bkrow", [1, D])
        k.dma("y", bqrow[:], bq_in.ap(), ld_misc)
        tok_ld0 = k.dma("y", bkrow[:], bk_in.ap(), ld_misc)

        k.wait("v", tok_ld0)
        k.op("v", lambda e: e.memset(ones_f[:], 1.0))
        k.op("v", lambda e: e.memset(ones_bf[:], 1.0))
        k.op("v", lambda e: e.memset(onescol_bf[:], 1.0))
        k.op("v", lambda e: e.memset(epsc[:], float(EPS)))
        v0 = k.op("v", lambda e: e.tensor_copy(bv16[:], bvrow[:]))

        k.wait("t", tok_ld0)
        k.wait("t", v0)
        t0 = k.op("t", lambda e: e.matmul(ps_bw[:, 0:48], ones_f[:], bwrow[:],
                                          start=True, stop=True), mm)
        tcols = None
        for c in range(NCH):
            k.op("t", lambda e, c=c: e.matmul(
                ps_b[0].ap()[:, c:c + 1], bqrow.ap()[0:1, c * 128:(c + 1) * 128],
                ones_f.ap()[0:1, 0:1], start=True, stop=True))
            tcols = k.op("t", lambda e, c=c: e.matmul(
                ps_b[1].ap()[:, c:c + 1], bkrow.ap()[0:1, c * 128:(c + 1) * 128],
                ones_f.ap()[0:1, 0:1], start=True, stop=True), mm)
        k.wait("v", tcols)
        k.op("v", lambda e: e.tensor_copy(bqraw[:], ps_b[0].ap()[:, 0:NCH]))
        vbk = k.op("v", lambda e: e.tensor_copy(bias_k[:], ps_b[1].ap()[:, 0:NCH]))
        k.wait("v", t0)
        k.op("v", lambda e: e.tensor_copy(bwb[:, 0:48], ps_bw[:, 0:48]))
        vM = None
        for h in range(NH):
            mh = m_sb.ap()[:, h * 128:(h + 1) * 128]
            g0 = g_sb.ap()[:, 0:128]
            g1 = g_sb.ap()[:, 128:256]
            g2 = g_sb.ap()[:, 256:384]
            g3 = g_sb.ap()[:, 384:512]
            s0 = bwb.ap()[:, h * 3:h * 3 + 1]
            s1 = bwb.ap()[:, h * 3 + 1:h * 3 + 2]
            s2 = bwb.ap()[:, h * 3 + 2:h * 3 + 3]
            k.op("v", lambda e, g0=g0, s0=s0: e.tensor_scalar_mul(tmp_sb[:], g0, s0))
            k.op("v", lambda e, mh=mh, g3=g3: e.tensor_add(mh, tmp_sb[:], g3))
            k.op("v", lambda e, g1=g1, s1=s1: e.tensor_scalar_mul(tmp_sb[:], g1, s1))
            k.op("v", lambda e, mh=mh: e.tensor_add(mh, mh, tmp_sb[:]))
            k.op("v", lambda e, g2=g2, s2=s2: e.tensor_scalar_mul(tmp_sb[:], g2, s2))
            vM = k.op("v", lambda e, mh=mh: e.tensor_add(mh, mh, tmp_sb[:]))

        k.wait("t", vM)
        k.wait("t", vbk)
        sq_toks = [None, None]
        for c in range(NCH):
            h = c // 2
            pb = ps_b[c % 2]
            k.wait("t", sq_toks[c % 2])
            tq = k.op("t", lambda e, pb=pb, h=h, c=c: e.matmul(
                pb[:, 0:1], m_sb.ap()[:, h * 128:(h + 1) * 128], bqraw.ap()[:, c:c + 1],
                start=True, stop=True), mm)
            k.wait("s", tq)
            sq_toks[c % 2] = k.op("s", lambda e, pb=pb, c=c: e.copy(bias_q[:, c:c + 1],
                                                                    pb[:, 0:1]))

        ld_w = [k.ch("ld_w%d" % i) for i in range(4)]
        st_w = [k.ch("st_w%d" % i) for i in range(4)]
        ld_f = [k.ch("ld_f0"), k.ch("ld_f1")]
        st_f = [k.ch("st_f0"), k.ch("st_f1")]
        cons_w = k.ch("cons_w")
        wblk = [sbal("wblk%d" % i, [128, 1024]) for i in range(4)]
        wst = [p0.enter_context(nc.sbuf_tensor("wst%d" % i, [128, 1024], BF16))
               for i in range(2)]
        fblk = [sbal("fblk0", [128, D]), sbal("fblk1", [128, D])]
        fst = [p0.enter_context(nc.sbuf_tensor("fst0", [128, D], BF16)),
               p0.enter_context(nc.sbuf_tensor("fst1", [128, D], BF16))]
        ps_wp = [psal("ps_wp%d" % i, [128, 1024]) for i in range(2)]
        warm_ctx["ps"] = psal("ps_warm1", [16, 16])

        fev_toks = [None, None]
        fst_toks = [None, None]
        for j in range(CIN_SH // 128):
            b = j % 2
            k.wait("y", fev_toks[b])
            lw = k.dma("y", fblk[b][:], wf_sl.ap()[j * 128:(j + 1) * 128, :], ld_f[b])
            k.wait("v", lw)
            k.wait("v", fst_toks[b])
            fev_toks[b] = k.op("v", lambda e, b=b: e.tensor_copy(fst[b][:], fblk[b][:]))
            k.wait("g", fev_toks[b])
            fst_toks[b] = k.dma("g", wf_loc.ap()[j * 128:(j + 1) * 128, :], fst[b][:],
                                st_f[b])
        k.waitv("g", st_f[0], st_f[0].n)
        k.waitv("g", st_f[1], st_f[1].n)
        k.op("g", lambda e: e.collective_compute(
            "AllGather", ALU.bypass, replica_groups=R8,
            ins=[wf_loc.ap().opt()], outs=[wf16.ap().opt()]), cc_wf)

        idx = 0
        ev_toks = [None] * 2
        st_toks = [None] * 2
        for wsrc, wdst, rhs_is_m, ccx in ((wq_sl, wqt_loc, True, cc_wq),
                                          (wk_sl, wkt_loc, False, cc_wk),
                                          (wv_sl, wvt_loc, False, cc_wv)):
            srcw2 = wsrc.ap().rearrange("(cc two p) n -> p cc two n", p=128, two=2)
            for c2 in range(NCH // 2):
                bw = idx % 4
                bp = idx % 2
                if idx >= 4:
                    k.waitv("y", cons_w, 8 * (idx - 3))
                lw = k.dma("y", wblk[bw].ap().rearrange("p (two n) -> p two n", two=2),
                           srcw2[:, c2], ld_w[bw])
                k.wait("t", lw)
                k.wait("t", ev_toks[bp])
                tm = None
                for two in range(2):
                    c = 2 * c2 + two
                    rhs = (m_sb.ap()[:, (c // 2) * 128:(c // 2 + 1) * 128]
                           if rhs_is_m else identf.ap())
                    for j in range(CIN_SH // 128):
                        tm = k.op("t", lambda e, bw=bw, bp=bp, j=j, two=two,
                                  rhs=rhs: e.matmul(
                            ps_wp[bp].ap()[:, two * 512 + j * 128: two * 512 + (j + 1) * 128],
                            wblk[bw].ap()[:, two * 512 + j * 128: two * 512 + (j + 1) * 128],
                            rhs, start=True, stop=True), cons_w)
                k.wait("v", tm)
                k.wait("v", st_toks[bp])
                ev_toks[bp] = k.op("v", lambda e, bp=bp: e.tensor_copy(wst[bp][:],
                                                                       ps_wp[bp][:]))
                k.wait("g", ev_toks[bp])
                dstap = wdst.ap().rearrange("(j p) n -> p j n", p=128)
                srcap = wst[bp].ap().rearrange("p (two j n) -> p two j n", two=2, n=128)
                for two in range(2):
                    c = 2 * c2 + two
                    st_toks[bp] = k.dma("g", dstap[:, :, c * 128:(c + 1) * 128],
                                        srcap[:, two], st_w[bp])
                idx += 1
            for chx in st_w:
                k.waitv("g", chx, chx.n)
            k.op("g", lambda e, a_=wsrc, w1=wdst, w2={"wqt_loc": wqt, "wkt_loc": wkt,
                 "wvt_loc": wvt}[wdst.name]: e.collective_compute(
                "AllGather", ALU.bypass, replica_groups=R8,
                ins=[w1.ap().opt()], outs=[w2.ap().opt()]), ccx)

    k.barrier()

    ld_x = [k.ch("ld_x0"), k.ch("ld_x1b")]
    ld_wr = [k.ch("ld_wr0"), k.ch("ld_wr1")]
    ld_h2 = k.ch("ld_h2")
    st_h = k.ch("st_h")
    tr = k.ch("tr")
    ld_xs = [k.ch("ld_xs0"), k.ch("ld_xs1")]
    st_x1 = [k.ch("st_x10"), k.ch("st_x11")]
    mm_f = k.ch("mm_f")

    def ln_front_a(st, src_d, half, tt):
        xb, hb, stt_ = st["xb"], st["hb"], st["stats"]
        bx = tt % 2
        off = 8 * bx
        row = half * HALF + tt * 128
        k.wait("y", st["xb_free"][bx])
        lw = k.dma("y", xb[bx][:], src_d.ap()[row:row + 128, :], ld_x[bx])
        k.wait("s", lw)
        k.wait("s", st["hb_free"][bx])
        st["a1"] = k.op("s", lambda e, bx=bx, off=off: e.activation(
            hb[bx][:], xb[bx][:], AF.Square, accum_out=stt_[:, off:off + 1]))

    def ln_front_b(st, src_d, half, tt):
        xb, hb, stt_ = st["xb"], st["hb"], st["stats"]
        bx = tt % 2
        off = 8 * bx
        a1 = st["a1"]
        k.wait("v", a1)
        k.op("v", lambda e, bx=bx, off=off: e.reduce_sum(stt_[:, off + 1:off + 2],
                                                         xb[bx][:], axis=AX.X))
        k.op("v", lambda e, off=off: e.tensor_scalar_mul(stt_[:, off + 2:off + 3],
                                                         stt_[:, off + 1:off + 2], 1.0 / D))
        k.op("v", lambda e, off=off: e.tensor_scalar_mul(stt_[:, off + 3:off + 4],
                                                         stt_[:, off:off + 1], 1.0 / D))
        k.op("v", lambda e, off=off: e.tensor_mul(stt_[:, off + 4:off + 5],
                                                  stt_[:, off + 2:off + 3],
                                                  stt_[:, off + 2:off + 3]))
        d1 = k.op("v", lambda e, off=off: e.tensor_sub(stt_[:, off + 3:off + 4],
                                                       stt_[:, off + 3:off + 4],
                                                       stt_[:, off + 4:off + 5]))
        k.wait("s", d1)
        k.op("s", lambda e, off=off: e.activation(stt_[:, off + 5:off + 6],
                                                  stt_[:, off + 3:off + 4], AF.Ln,
                                                  bias=epsc[:, 0:1]))
        a2 = k.op("s", lambda e, off=off: e.activation(stt_[:, off + 6:off + 7],
                                                       stt_[:, off + 5:off + 6], AF.Exp,
                                                       scale=-0.5))
        k.wait("v", a2)
        k.op("v", lambda e, off=off: e.tensor_mul(stt_[:, off + 7:off + 8],
                                                  stt_[:, off + 2:off + 3],
                                                  stt_[:, off + 6:off + 7]))
        d2 = k.op("v", lambda e, off=off: e.tensor_scalar_mul(stt_[:, off + 7:off + 8],
                                                              stt_[:, off + 7:off + 8],
                                                              -1.0))
        k.wait("s", d2)
        a3 = k.op("s", lambda e, bx=bx, off=off: e.activation(
            hb[bx][:], xb[bx][:], AF.Identity,
            bias=stt_[:, off + 7:off + 8], scale=stt_[:, off + 6:off + 7]))
        st["xb_free"][bx] = a3
        st["a3"][bx] = a3

    def ln_back(st, half, tt, h1T, dst_dram):
        hb, h2st, ps_tr = st["hb"], st["h2st"], st["ps_tr"]
        bx = tt % 2
        k.wait("t", st["a3"][bx])
        for c in range(NCH):
            pb = c % 4
            k.wait("t", st["ps_tr_free"][pb])
            tv = k.op("t", lambda e, pb=pb, bx=bx, c=c: e.matmul(
                ps_tr[pb][:], hb[bx].ap()[:, c * 128:(c + 1) * 128], ident16[:],
                start=True, stop=True), tr)
            eng = "v" if c % 4 < 3 else "s"
            k.wait(eng, tv)
            if h1T is not None:
                dst = h1T.ap()[:, c * HALF + tt * 128: c * HALF + tt * 128 + 128]
            else:
                k.wait(eng, st["h2st_free"])
                dst = h2st.ap()[:, c * 128:(c + 1) * 128]
            if eng == "v":
                ctok = k.op("v", lambda e, dst=dst, pb=pb: e.tensor_copy(dst,
                                                                         ps_tr[pb][:]))
            else:
                ctok = k.op("s", lambda e, dst=dst, pb=pb: e.copy(dst, ps_tr[pb][:]))
            st["ps_tr_free"][pb] = ctok
            st["cp_last"][eng] = ctok
        st["hb_free"][bx] = (tr, tr.n)
        if h1T is None:
            k.wait("g", st["cp_last"]["v"])
            k.wait("g", st["cp_last"]["s"])
            col = half * HALF + tt * 128
            dstap = dst_dram.ap().rearrange("(c p) m -> p c m", p=128)
            srcap = h2st.ap().rearrange("p (c n) -> p c n", n=128)
            htok = None
            for q4 in range(4):
                htok = k.dma("g", dstap[:, q4 * 8:(q4 + 1) * 8, col:col + 128],
                             srcap[:, q4 * 8:(q4 + 1) * 8, :], st_h)
            st["h2st_free"] = htok
        st["ln_i"] += 1

    def ln_pass(st, src_d, half, h1T=None, dst_dram=None):
        for tt in range(9):
            if tt < 8:
                ln_front_a(st, src_d, half, tt)
            if tt >= 1:
                ln_back(st, half, tt - 1, h1T, dst_dram)
            if tt < 8:
                ln_front_b(st, src_d, half, tt)

    def ffn_half(st, xin_d, xout_d, half, wsrc, first_use_wait=None):
        wres, xsl, x1st, ps_f, h1T = st["wres"], st["xsl"], st["x1st"], st["ps_f"], st["h1T"]
        if first_use_wait is not None:
            k.wait("y", first_use_wait)
        k.waitv("t", sv, sv.n)
        k.waitv("t", ss, ss.n)
        for co in range(8):
            b = co % 2
            k.wait("y", st["wres_free"][b])
            srcap = wsrc.ap().rearrange("(j p) n -> p j n", p=128)
            dstap = wres[b].ap().rearrange("p (j n) -> p j n", n=512)
            lw = None
            for q4 in range(4):
                lw = k.dma("y", dstap[:, q4 * 8:(q4 + 1) * 8, :],
                           srcap[:, q4 * 8:(q4 + 1) * 8, co * 512:(co + 1) * 512], ld_wr[b])
            k.wait("t", lw)
            for tt in range(8):
                gi = st["ffn_i"] * 8 + tt
                pb = gi % 3
                k.wait("t", st["psf_free"][pb])
                tmf = None
                for j in range(NCH):
                    tmf = k.op("t", lambda e, pb=pb, b=b, j=j, tt=tt: e.matmul(
                        ps_f[pb][:],
                        h1T.ap()[:, j * HALF + tt * 128: j * HALF + tt * 128 + 128],
                        wres[b].ap()[:, j * 512:(j + 1) * 512],
                        start=(j == 0), stop=(j == NCH - 1)),
                        mm_f if j == NCH - 1 else None)
                bx = gi % 2
                k.wait("y", st["xsl_free"][bx])
                row = half * HALF + tt * 128
                lx = k.dma("y", xsl[bx][:],
                           xin_d.ap()[row:row + 128, co * 512:(co + 1) * 512], ld_xs[bx])
                k.wait("v", tmf)
                k.wait("v", lx)
                k.wait("v", st["x1st_free"][bx])
                ev = k.op("v", lambda e, pb=pb, bx=bx: e.scalar_tensor_tensor(
                    x1st[bx][:], ps_f[pb][:], 0.5, xsl[bx][:], ALU.mult, ALU.add))
                st["psf_free"][pb] = ev
                st["xsl_free"][bx] = ev
                k.wait("g", ev)
                st["x1st_free"][bx] = k.dma(
                    "g", xout_d.ap()[row:row + 128, co * 512:(co + 1) * 512],
                    x1st[bx][:], st_x1[bx])
            st["wres_free"][b] = (mm_f, mm_f.n)
            st["ffn_i"] += 1

    def ffn_phase(xin_d, xout_d, ln_out_dram, wsrc, first_use_wait, tag):
        with ExitStack() as px:
            sbal = lambda n_, s_, d_=F32: px.enter_context(nc.sbuf_tensor(tag + n_, s_, d_))
            st = {
                "xb": [sbal("xb0", [128, D]), sbal("xb1", [128, D])],
                "hb": [sbal("hb0", [128, D], BF16), sbal("hb1", [128, D], BF16)],
                "stats": sbal("stats", [128, 16]),
                "h2st": sbal("h2st", [128, NCH * 128], BF16),
                "h1T": sbal("h1T", [128, NCH * HALF], BF16),
                "wres": [sbal("wres0", [128, NCH * 512], BF16),
                         sbal("wres1", [128, NCH * 512], BF16)],
                "xsl": [sbal("xsl0", [128, 512]), sbal("xsl1", [128, 512])],
                "x1st": [sbal("x1st0", [128, 512]), sbal("x1st1", [128, 512])],
                "ps_tr": [px.enter_context(nc.psum_tensor(tag + "ps_tr%d" % i,
                                                           [128, 128], F32))
                          for i in range(4)],
                "ps_f": [px.enter_context(nc.psum_tensor(tag + "ps_f%d" % i,
                                                          [128, 512], F32))
                         for i in range(3)],
                "ps_warm": px.enter_context(nc.psum_tensor(tag + "ps_warm", [16, 16], F32)),
                "ln_i": 0, "ffn_i": 0,
                "xb_free": [None, None], "hb_free": [None, None],
                "psf_free": [None, None, None], "xsl_free": [None, None],
                "x1st_free": [None, None], "wres_free": [None, None],
                "ps_tr_free": [None, None, None, None], "h2st_free": None,
                "a3": [None, None],
                "cp_last": {"v": None, "s": None},
            }
            warm_ctx["ps"] = st["ps_warm"]
            for half in (0, 1):
                ln_pass(st, xin_d, half, h1T=st["h1T"])
                ffn_half(st, xin_d, xout_d, half, wsrc,
                         first_use_wait=first_use_wait if st["ffn_i"] == 0 else None)
                if ln_out_dram is not None:
                    k.waitv("y", st_x1[0], st_x1[0].n)
                    k.waitv("y", st_x1[1], st_x1[1].n)
                    ln_pass(st, xout_d, half, h1T=None, dst_dram=ln_out_dram)
            k.barrier()

    ffn_phase(x_in, x1_d, h2t_d, wf16, first_use_wait=(cc_wf, 1), tag="p2_")

    ld_w2 = [k.ch("ld_w2%d" % i) for i in range(6)]
    mm_k = k.ch("mm_k")
    st_kv = [k.ch("st_kv0"), k.ch("st_kv1")]
    mm_n = k.ch("mm_n")
    ld_n = [k.ch("ld_n0"), k.ch("ld_n1")]
    ld_kn = k.ch("ld_kn")

    with ExitStack() as p3:
        sbal = lambda n_, s_, d_=F32: p3.enter_context(nc.sbuf_tensor(n_, s_, d_))
        h2res = p3.enter_context(nc.sbuf_tensor("h2res", [128, NCH * HALF], BF16))
        wslab = [p3.enter_context(nc.sbuf_tensor("wslab%d" % i, [128, NCH * 256], BF16))
                 for i in range(6)]
        kst = [p3.enter_context(nc.sbuf_tensor("kst0", [128, 512], BF16)),
               p3.enter_context(nc.sbuf_tensor("kst1", [128, 512], BF16))]
        sqs = [p3.enter_context(nc.sbuf_tensor("sqs0", [128, 512], BF16)),
               p3.enter_context(nc.sbuf_tensor("sqs1", [128, 512], BF16))]
        ntmp = sbal("ntmp", [1, 8])
        nrm = sbal("nrm", [16, 8])
        qmaxr = sbal("qmaxr", [1, NH])
        kmaxr = sbal("kmaxr", [1, NH])
        knsb = sbal("knsb", [16, 4])
        nb1 = sbal("nb1", [1, NH])
        ps_kv = [p3.enter_context(nc.psum_tensor("ps_kv%d" % i, [128, 512], F32))
                 for i in range(3)]
        ps_n = [p3.enter_context(nc.psum_tensor("ps_n%d" % i, [1, 512], F32))
                for i in range(4)]
        ps_x = p3.enter_context(nc.psum_tensor("ps_x", [128, 64], F32))
        warm_ctx["ps"] = None

        st3 = {"wslab_free": [None] * 6, "pskv_free": [None, None, None],
               "kst_free": [None, None], "wi": 0, "pre": {}, "pv": 0,
               "psn_free": [None, None, None, None], "nrm_last": None,
               "pskv3": True,
               "sq_free": [None, None]}

        def load_h2res(half):
            srcap = h2t_d.ap().rearrange("(j p) m -> p j m", p=128)
            dstap = h2res.ap().rearrange("p (j m) -> p j m", m=HALF)
            lw = None
            for q4 in range(4):
                lw = k.dma("y", dstap[:, q4 * 8:(q4 + 1) * 8, :],
                           srcap[:, q4 * 8:(q4 + 1) * 8, half * HALF:(half + 1) * HALF],
                           ld_h2)
            return lw

        def slab_load_kq(wsrc, c):
            b = st3["wi"] % 6
            st3["wi"] += 1
            srcw = wsrc.ap().rearrange("(j p) n -> p j n", p=128)
            k.wait("y", st3["wslab_free"][b])
            dstw = wslab[b].ap().rearrange("p (j n) -> p j n", n=256)[:, :, 0:128]
            lw = None
            for q4 in range(4):
                lw = k.dma("y", dstw[:, q4 * 8:(q4 + 1) * 8, :],
                           srcw[:, q4 * 8:(q4 + 1) * 8, c * 128:(c + 1) * 128], ld_w2[b])
            return b, lw

        def slab_load_v(h):
            b = st3["wi"] % 6
            st3["wi"] += 1
            srcw = wvt.ap().rearrange("(j p) n -> p j n", p=128)
            k.wait("y", st3["wslab_free"][b])
            dstw = wslab[b].ap().rearrange("p (j n) -> p j n", n=256)
            lw = None
            for q4 in range(4):
                lw = k.dma("y", dstw[:, q4 * 8:(q4 + 1) * 8, :],
                           srcw[:, q4 * 8:(q4 + 1) * 8, h * 256:(h + 1) * 256], ld_w2[b])
            return b, lw

        def kq_pass(half, wsrc, bias_sb, dst_dram, h2w, maxrow, split):
            for c in range(NCH):
                if ("kq", half, c) in st3["pre"]:
                    b, lw = st3["pre"].pop(("kq", half, c))
                else:
                    b, lw = slab_load_kq(wsrc, c)
                k.wait("t", lw)
                k.wait("t", h2w)
                sq_toks = [None, None]
                for t2 in range(2):
                    pb = st3["pv"] % 3
                    st3["pv"] += 1
                    k.wait("t", st3["pskv_free"][pb])
                    tmk = None
                    for j in range(NCH):
                        tmk = k.op("t", lambda e, pb=pb, b=b, j=j, t2=t2: e.matmul(
                            ps_kv[pb][:], wslab[b].ap()[:, j * 256:j * 256 + 128],
                            h2res.ap()[:, j * HALF + t2 * 512: j * HALF + t2 * 512 + 512],
                            start=(j == 0), stop=(j == NCH - 1)),
                            mm_k if j == NCH - 1 else None)
                    gb = t2
                    k.wait("s", tmk)
                    k.wait("s", st3["kst_free"][gb])
                    ev = k.op("s", lambda e, pb=pb, c=c, gb=gb, bias_sb=bias_sb:
                              e.activation(kst[gb][:], ps_kv[pb][:], AF.Identity,
                                           bias=bias_sb.ap()[:, c:c + 1]))
                    st3["pskv_free"][pb] = ev
                    k.wait("g", ev)
                    if split:
                        ro = half * D + c * 128
                        col = t2 * 512
                    else:
                        ro = c * 128
                        col = half * HALF + t2 * 512
                    st3["kst_free"][gb] = k.dma(
                        "g", dst_dram.ap()[ro:ro + 128, col:col + 512],
                        kst[gb][:], st_kv[gb])
                    k.wait("s", st3["sq_free"][gb])
                    sq_toks[t2] = k.op("s", lambda e, gb=gb: e.activation(
                        sqs[gb][:], kst[gb][:], AF.Square))
                for t2 in range(2):
                    g_ = 2 * half + t2
                    k.wait("t", sq_toks[t2])
                    if c % 2 == 0:
                        k.wait("t", st3["psn_free"][g_])
                    tn = k.op("t", lambda e, g_=g_, t2=t2, c=c: e.matmul(
                        ps_n[g_][:], onescol_bf[:], sqs[t2].ap()[:, 0:512],
                        start=(c % 2 == 0), stop=(c % 2 == 1)), mm_k)
                    st3["sq_free"][t2] = (mm_k, mm_k.n)
                    if c % 2 == 1:
                        h = c // 2
                        k.wait("v", tn)
                        k.op("v", lambda e, g_=g_: e.reduce_max(ntmp.ap()[0:1, g_:g_ + 1],
                                                                ps_n[g_][:], axis=AX.X))
                        vt = k.op("v", lambda e, g_=g_, h=h, maxrow=maxrow: e.tensor_max(
                            maxrow.ap()[0:1, h:h + 1], maxrow.ap()[0:1, h:h + 1],
                            ntmp.ap()[0:1, g_:g_ + 1]))
                        st3["psn_free"][g_] = vt
                        st3["nrm_last"] = vt
                st3["wslab_free"][b] = (mm_k, mm_k.n)

        def v_pass(half, h2w):
            k.waitv("v", ss, ss.n)
            for h in range(NH):
                if ("v", half, h) in st3["pre"]:
                    b, lw = st3["pre"].pop(("v", half, h))
                else:
                    b, lw = slab_load_v(h)
                k.wait("t", lw)
                k.wait("t", h2w)
                for tt in range(8):
                    pb = st3["pv"] % 3
                    st3["pv"] += 1
                    k.wait("t", st3["pskv_free"][pb])
                    for j in range(NCH):
                        k.op("t", lambda e, pb=pb, b=b, j=j, tt=tt: e.matmul(
                            ps_kv[pb][:, 0:256],
                            h2res.ap()[:, j * HALF + tt * 128: j * HALF + tt * 128 + 128],
                            wslab[b].ap()[:, j * 256:(j + 1) * 256],
                            start=(j == 0), stop=False))
                    tmk = k.op("t", lambda e, pb=pb, h=h: e.matmul(
                        ps_kv[pb][:, 0:256], ones_bf[:], bv16.ap()[:, h * 256:(h + 1) * 256],
                        start=False, stop=True), mm_k)
                    gb = tt % 2
                    k.wait("v", tmk)
                    k.wait("v", st3["kst_free"][gb])
                    ev = k.op("v", lambda e, pb=pb, gb=gb: e.tensor_copy(
                        kst[gb][:, 0:256], ps_kv[pb][:, 0:256]))
                    st3["pskv_free"][pb] = ev
                    k.wait("g", ev)
                    row = half * NH * HALF + h * HALF + tt * 128
                    st3["kst_free"][gb] = k.dma("g", v_loc.ap()[row:row + 128, :],
                                                kst[gb][:, 0:256], st_kv[gb])
                st3["wslab_free"][b] = (mm_k, mm_k.n)

        mz = k.op("v", lambda e: e.memset(kmaxr[:], 0.0))
        mz = k.op("v", lambda e: e.memset(qmaxr[:], 0.0))
        for half in (0, 1):
            if half == 1:
                k.waitv("y", mm_k, mm_k.n)
            h2w = load_h2res(half)
            if half == 0:
                k.waitv("y", cc_wk, 1)
                k.waitv("y", cc_wv, 1)
            kq_pass(half, wkt, bias_k, k_loc, h2w, kmaxr, split=True)
            k.waitv("g", st_kv[0], st_kv[0].n)
            k.waitv("g", st_kv[1], st_kv[1].n)
            k.op("g", lambda e, half=half: e.collective_compute(
                "AllGather", ALU.bypass, replica_groups=R4,
                ins=[k_loc.ap()[half * D:(half + 1) * D, :].opt()],
                outs=[k_full[half].ap().opt()]), cc_k[half])
        ktok = st3["nrm_last"]
        k.wait("g", ktok)
        kstn = k.dma("g", kn_loc.ap(), kmaxr[:], st_n)
        k.wait("g", kstn)
        k.op("g", lambda e: e.collective_compute(
            "AllGather", ALU.bypass, replica_groups=R4,
            ins=[kn_loc.ap().opt()], outs=[kn_g.ap().opt()]), cc_n)

        for half in (0, 1):
            if half == 0:
                for h in range(3):
                    st3["pre"][("v", 0, h)] = slab_load_v(h)
            k.waitv("y", mm_k, mm_k.n)
            h2w = load_h2res(half)
            v_pass(half, h2w)
            k.waitv("g", st_kv[0], st_kv[0].n)
            k.waitv("g", st_kv[1], st_kv[1].n)
            k.op("g", lambda e, half=half: e.collective_compute(
                "AllGather", ALU.bypass, replica_groups=R4,
                ins=[v_loc.ap()[half * NH * HALF:(half + 1) * NH * HALF, :].opt()],
                outs=[v_full[half].ap().opt()]), cc_v[half])

        for half in (0, 1):
            if half == 0:
                k.waitv("y", cc_wq, 1)
                for c in range(3):
                    st3["pre"][("kq", 0, c)] = slab_load_kq(wqt, c)
            k.waitv("y", mm_k, mm_k.n)
            h2w = load_h2res(half)
            kq_pass(half, wqt, bias_q, q_loc, h2w, qmaxr, split=False)
        qtok = st3["nrm_last"]

        k.waitv("y", cc_n, 1)
        kln = None
        for r in range(4):
            kln = k.dma("y", knsb[:, r:r + 1],
                        kn_g.ap()[r:r + 1, :].rearrange("one h -> h one"), ld_kn)

        k.wait("v", kln)
        k.op("v", lambda e: e.reduce_max(nrm.ap()[0:16, 4:5], knsb[:], axis=AX.X))
        k.wait("t", qtok)
        tq1 = k.op("t", lambda e: e.transpose(ps_x[0:16, 0:1], qmaxr[:],
                                              identf.ap()[0:1, 0:1]), mm)
        k.wait("v", tq1)
        k.op("v", lambda e: e.tensor_copy(nrm.ap()[0:16, 5:6], ps_x[0:16, 0:1]))
        d4 = k.op("v", lambda e: e.tensor_mul(nrm.ap()[0:16, 6:7], nrm.ap()[0:16, 4:5],
                                              nrm.ap()[0:16, 5:6]))
        k.wait("s", d4)
        a4 = k.op("s", lambda e: e.activation(nrm.ap()[0:16, 7:8], nrm.ap()[0:16, 6:7],
                                              AF.Sqrt))
        k.wait("v", a4)
        d5 = k.op("v", lambda e: e.tensor_scalar_mul(nrm.ap()[0:16, 7:8],
                                                     nrm.ap()[0:16, 7:8], -1.0))
        k.wait("t", d5)
        tq2 = k.op("t", lambda e: e.transpose(ps_x[0:1, 0:16], nrm.ap()[0:16, 7:8],
                                              identf.ap()[0:16, 0:16]), mm)
        k.wait("v", tq2)
        d6 = k.op("v", lambda e: e.tensor_copy(nb1[:], ps_x[0:1, 0:16]))
        k.wait("t", d6)
        tq3 = k.op("t", lambda e: e.matmul(ps_x[:, 0:16], ones_f[:], nb1[:],
                                           start=True, stop=True), mm)
        k.wait("v", tq3)
        k.op("v", lambda e: e.tensor_copy(expb[:], ps_x[:, 0:16]))
    k.barrier()

    ld_k4 = [k.ch("ld_k40"), k.ch("ld_k41")]
    ld_v4 = [k.ch("ld_v40"), k.ch("ld_v41")]
    ld_x1 = [k.ch("ld_x1a"), k.ch("ld_x1c")]
    st_x2 = [k.ch("st_x20"), k.ch("st_x21")]
    tl = k.ch("tl")
    av = k.ch("av")

    with ExitStack() as p4:
        kh = [p4.enter_context(nc.sbuf_tensor("kh0", [128, 2 * LK], BF16)),
              p4.enter_context(nc.sbuf_tensor("kh1", [128, 2 * LK], BF16))]
        vh = [p4.enter_context(nc.sbuf_tensor("vh0", [128, MCH * 257], BF16)),
              p4.enter_context(nc.sbuf_tensor("vh1", [128, MCH * 257], BF16))]
        qh = [p4.enter_context(nc.sbuf_tensor("qh0", [128, 2 * TOK], BF16)),
              p4.enter_context(nc.sbuf_tensor("qh1", [128, 2 * TOK], BF16))]
        p_sb = [p4.enter_context(nc.sbuf_tensor("p0", [128, 1024], BF16)),
                p4.enter_context(nc.sbuf_tensor("p1", [128, 1024], BF16))]
        x1sl = [p4.enter_context(nc.sbuf_tensor("x1sl0", [128, 256], F32)),
                p4.enter_context(nc.sbuf_tensor("x1sl1", [128, 256], F32))]
        x2st = [p4.enter_context(nc.sbuf_tensor("x2st0", [128, 256], F32)),
                p4.enter_context(nc.sbuf_tensor("x2st1", [128, 256], F32))]
        rcp = p4.enter_context(nc.sbuf_tensor("rcp", [128, 4], F32))
        ps_l = [p4.enter_context(nc.psum_tensor("ps_l0", [128, 1024], F32)),
                p4.enter_context(nc.psum_tensor("ps_l1", [128, 1024], F32))]
        ps_av4 = p4.enter_context(nc.psum_tensor("ps_av4", [128, 2048], F32))
        ps_av = [ps_av4.ap()[:, i * 512:(i + 1) * 512] for i in range(4)]

        k.op("v", lambda e: e.memset(
            vh[0].ap().rearrange("p (c n) -> p c n", n=257)[:, :, 256:257], 1.0))
        vones = k.op("v", lambda e: e.memset(
            vh[1].ap().rearrange("p (c n) -> p c n", n=257)[:, :, 256:257], 1.0))

        def load_head(h, hb):
            ktok_ = None
            for fc in range(2):
                for r in range(4):
                    for hf in range(2):
                        ktok_ = k.dma(
                            "y",
                            kh[hb][:, fc * LK + r * TOK + hf * HALF:
                                   fc * LK + r * TOK + (hf + 1) * HALF],
                            k_full[hf].ap()[r * D + h * 256 + fc * 128:
                                            r * D + h * 256 + fc * 128 + 128, :],
                            ld_k4[hb])
            for fc in range(2):
                ktok_ = k.dma("y", qh[hb][:, fc * TOK:(fc + 1) * TOK],
                              q_loc.ap()[h * 256 + fc * 128: h * 256 + fc * 128 + 128, :],
                              ld_k4[hb])
            vtok_ = None
            for r in range(4):
                for hf in range(2):
                    for s2 in range(2):
                        base = (r * NH + h) * HALF + s2 * 512
                        cbase = r * 16 + hf * 8 + s2 * 4
                        dst = vh[hb].ap().rearrange("p (c n) -> p c n", n=257)[
                            :, cbase:cbase + 4, 0:256]
                        src = v_full[hf].ap()[base:base + 512, :].rearrange(
                            "(c p) n -> p c n", p=128)
                        vtok_ = k.dma("g", dst, src, ld_v4[hb])
            return ktok_, vtok_

        for hf in range(2):
            k.waitv("y", cc_k[hf], 1)
            k.waitv("y", cc_v[hf], 1)
            k.waitv("g", cc_k[hf], 1)
            k.waitv("g", cc_v[hf], 1)
        k.wait("g", vones)
        marks = {0: load_head(0, 0)}
        x1_free = [None, None]
        x2_free = [None, None]
        exp_toks = {}
        ev4_last = None

        mc_glob = 0
        for h in range(NH):
            hb = h % 2
            k.wait("t", marks[h][0])
            k.wait("t", marks[h][1])
            if h == 0:
                k.wait("t", vones)
            if h + 1 < NH:
                if h >= 1:
                    k.waitv("y", av, 128 * h)
                    k.waitv("g", av, 128 * h)
                marks[h + 1] = load_head(h + 1, (h + 1) % 2)
            for lg in range(4):
                r0 = mc_glob // 2
                NR = MCH // 2
                for r in range(NR):
                    ri = r0 + r
                    pb = ri % 2
                    k.wait("t", exp_toks.pop(ri - 2, None))
                    for jj in range(2):
                        j = 2 * r + jj
                        for fc in range(2):
                            k.op("t", lambda e, pb=pb, hb=hb, fc=fc, j=j, jj=jj,
                                 lg=lg: e.matmul(
                                ps_l[pb].ap()[:, jj * 512:(jj + 1) * 512],
                                kh[hb].ap()[:, fc * LK + j * 128: fc * LK + j * 128 + 128],
                                qh[hb].ap()[:, fc * TOK + lg * 512: fc * TOK + lg * 512 + 512],
                                start=(fc == 0), stop=(fc == 1)),
                                tl if (fc == 1 and jj == 1) else None)
                    if r >= 1:
                        k.wait("t", exp_toks.get(ri - 1))
                        if r == 1:
                            k.wait("t", ev4_last)
                        for jj in range(2):
                            jm = 2 * (r - 1) + jj
                            for lc in range(4):
                                k.op("t", lambda e, lc=lc, hb=hb, jm=jm, jj=jj,
                                     pp=(ri - 1) % 2: e.matmul(
                                    ps_av[lc][:, 0:257],
                                    p_sb[pp].ap()[:, jj * 512 + lc * 128: jj * 512 + (lc + 1) * 128],
                                    vh[hb].ap()[:, jm * 257:(jm + 1) * 257],
                                    start=(jm == 0), stop=False),
                                    av if (lc == 3 and jj == 1) else None)
                    k.waitv("s", tl, ri + 1)
                    k.waitv("s", av, max(0, ri - 1))
                    exp_toks[ri] = k.op("s", lambda e, pb=pb, h=h: e.activation(
                        p_sb[pb][:], ps_l[pb][:], AF.Exp, bias=expb.ap()[:, h:h + 1]))
                k.wait("t", exp_toks.get(r0 + NR - 1))
                for jj in range(2):
                    jm = MCH - 2 + jj
                    for lc in range(4):
                        k.op("t", lambda e, lc=lc, hb=hb, jm=jm, jj=jj,
                             pp=(r0 + NR - 1) % 2: e.matmul(
                            ps_av[lc][:, 0:257],
                            p_sb[pp].ap()[:, jj * 512 + lc * 128: jj * 512 + (lc + 1) * 128],
                            vh[hb].ap()[:, jm * 257:(jm + 1) * 257],
                            start=False, stop=(jm == MCH - 1)),
                            av if (lc == 3 and jj == 1) else None)
                mc_glob += MCH
                for lc in range(4):
                    bx = lc % 2
                    row = lg * 512 + lc * 128
                    k.wait("y", x1_free[bx])
                    lx = k.dma("y", x1sl[bx][:],
                               x1_d.ap()[row:row + 128, h * 256:(h + 1) * 256], ld_x1[bx])
                    if lc == 0:
                        k.waitv("v", av, av.n)
                        k.op("v", lambda e: e.reciprocal(
                            rcp[:], ps_av4.ap().rearrange("p (lc n) -> p lc n", n=512)[
                                :, :, 256:257].rearrange("p lc one -> p (lc one)")))
                    k.wait("v", lx)
                    k.wait("v", x2_free[bx])
                    ev = k.op("v", lambda e, lc=lc, bx=bx: e.scalar_tensor_tensor(
                        x2st[bx][:], ps_av[lc][:, 0:256], rcp[:, lc:lc + 1], x1sl[bx][:],
                        ALU.mult, ALU.add))
                    x1_free[bx] = ev
                    ev4_last = ev
                    k.wait("g", ev)
                    x2_free[bx] = k.dma("g", x2_d.ap()[row:row + 128,
                                                       h * 256:(h + 1) * 256],
                                        x2st[bx][:], st_x2[bx])
    k.barrier()

    ffn_phase(x2_d, out_d, None, wf16, first_use_wait=None, tag="p5_")

    k.waitv("g", st_x1[0], st_x1[0].n)
    k.waitv("g", st_x1[1], st_x1[1].n)
    k.waitv("y", st_x1[0], st_x1[0].n)
    k.waitv("y", st_x1[1], st_x1[1].n)

    with nc.Block() as block:
        @block.sync
        def _(e):
            for f in k.rec["y"]:
                f(e)

        @block.tensor
        def _(e):
            for f in k.rec["t"]:
                f(e)

        @block.vector
        def _(e):
            for f in k.rec["v"]:
                f(e)

        @block.scalar
        def _(e):
            for f in k.rec["s"]:
                f(e)

        @block.gpsimd
        def _(e):
            for f in k.rec["g"]:
                f(e)

    top.close()
    return nc


def kernel(L0_attn_x, L0_attn_Wq, L0_attn_bq, L0_attn_Wk, L0_attn_bk,
           L0_attn_Wv, L0_attn_bv, L0_attn_bias_weight, L0_ffn_weight):
    if "nc" not in _CACHE:
        _CACHE["nc"] = build_nc()
    nc = _CACHE["nc"]

    x = np.ascontiguousarray(np.asarray(L0_attn_x, dtype=np.float32)).reshape(16384, D)
    Wq = np.asarray(L0_attn_Wq, dtype=np.float32)
    Wk = np.asarray(L0_attn_Wk, dtype=np.float32)
    Wv = np.asarray(L0_attn_Wv, dtype=np.float32)
    Wf = np.asarray(L0_ffn_weight, dtype=np.float32)
    bq = np.asarray(L0_attn_bq, dtype=np.float32).reshape(1, D)
    bk = np.asarray(L0_attn_bk, dtype=np.float32).reshape(1, D)
    bv = np.asarray(L0_attn_bv, dtype=np.float32).reshape(1, D)
    bw = np.asarray(L0_attn_bias_weight, dtype=np.float32).reshape(1, NH * 3)

    in_maps = []
    for i in range(NCORES):
        sl = slice(i * CIN_SH, (i + 1) * CIN_SH)
        in_maps.append({
            "x": np.ascontiguousarray(x[i * TOK:(i + 1) * TOK]),
            "wq_sl": np.ascontiguousarray(Wq[:, sl]),
            "wk_sl": np.ascontiguousarray(Wk[:, sl]),
            "wv_sl": np.ascontiguousarray(Wv[:, sl]),
            "wf_sl": np.ascontiguousarray(Wf[sl, :]),
            "bq": bq, "bk": bk, "bv": bv, "bw": bw,
        })

    res = run_bass_kernel_spmd(nc, in_maps, core_ids=list(range(NCORES)))
    out = np.concatenate([r["out"] for r in res.results], axis=0)
    return out.reshape(2, 8192, D)


# version 17 (final)
# speedup vs baseline: 1.0541x; 1.0261x over previous
import numpy as np
import ml_dtypes
from contextlib import ExitStack

import concourse.bass as bass
import concourse.mybir as mybir
from concourse.bass_utils import run_bass_kernel_spmd

F32 = mybir.dt.float32
BF16 = mybir.dt.bfloat16
AF = mybir.ActivationFunctionType
ALU = mybir.AluOpType
AX = mybir.AxisListType

NCORES = 8
TOK = 2048
D = 4096
NCH = 32
NH = 16
LK = 8192
MCH = LK // 128
HALF = 1024
CIN_SH = D // NCORES
EPS = 1e-5
SCALE = 1.0 / np.sqrt(256.0)

R8 = [[0, 1, 2, 3, 4, 5, 6, 7]]
R4 = [[0, 1, 2, 3], [4, 5, 6, 7]]

_CACHE = {}


def _build_tables():
    g = [1.0, 1.0, 1.0]
    dim, n = 3, 8
    masks = sorted(range(n), key=lambda m: (bin(m).count("1"), m))
    inv = {m: i for i, m in enumerate(masks)}
    C = np.zeros((n, n, n), dtype=np.float32)
    for i, a in enumerate(masks):
        for j, b in enumerate(masks):
            sw, at = 0, a >> 1
            while at:
                sw += bin(at & b).count("1")
                at >>= 1
            s = -1.0 if sw & 1 else 1.0
            for t in range(dim):
                if a >> t & 1 and b >> t & 1:
                    s *= float(g[t])
            C[i, j, inv[a ^ b]] = s
    grades = np.array([bin(m).count("1") for m in masks])
    rev = np.where(grades * (grades - 1) // 2 % 2 == 1, -1.0, 1.0).astype(np.float32)
    met = np.array([float(np.prod([g[t] for t in range(dim) if m >> t & 1] + [1.0]))
                    for m in masks], dtype=np.float32)
    return C, rev, met


def _g_const():
    C, rev, met = _build_tables()
    bivk = C[:, :, [4, 5, 6]]
    G = np.zeros((4, 128, 128), dtype=np.float32)
    for d in range(16):
        for i in range(8):
            for j in range(8):
                for c in range(3):
                    G[c, d * 8 + i, d * 8 + j] = rev[i] * bivk[i, j, c]
            G[3, d * 8 + i, d * 8 + i] = SCALE * rev[i] * met[i]
    return G


class Ch:
    __slots__ = ("sem", "n", "name")

    def __init__(self, sem, name):
        self.sem = sem
        self.n = 0
        self.name = name

    def bump(self, k=1):
        self.n += k
        return self.n


class K:

    def __init__(self, nc):
        self.nc = nc
        self.rec = {e: [] for e in ("y", "t", "v", "s", "g")}
        self._chs = {}
        self.owner = {}
        self._burst = {e: None for e in ("y", "t", "v", "s", "g")}
        self.selfch = {"v": self.ch("sv"), "s": self.ch("ss")}

    def ch(self, name):
        if name not in self._chs:
            self._chs[name] = Ch(self.nc.alloc_semaphore(name), name)
        return self._chs[name]

    def _raw(self, eng, fn):
        self.rec[eng].append(fn)

    def on(self, eng, fn):
        self._burst[eng] = None
        self.rec[eng].append(fn)

    def wait(self, eng, tok):
        if tok is None:
            return
        def f(e, t=tok):
            v = int(t[1])
            if v > 0:
                e.wait_ge(t[0].sem, v)
        self.on(eng, f)

    def waitv(self, eng, ch, val):
        b = None
        for e_, bb in self._burst.items():
            if bb is not None and bb[0] is ch:
                b = bb
                break
        if b is not None and int(val) == ch.n:
            self.wait(eng, b[1][-1])
        else:
            self.wait(eng, [ch, val])

    def op(self, eng, fn, ch=None, k=1):
        if eng in ("v", "s"):
            assert ch is None
            sc = self.selfch[eng]
            if sc.n > 0:
                def w(e, s=sc.sem, v=sc.n):
                    e.wait_ge(s, v)
                self.on(eng, w)
            else:
                self._burst[eng] = None
            v2 = sc.bump(1)
            self._raw(eng, lambda e, f=fn, s=sc.sem: f(e).then_inc(s, 1))
            return [sc, v2]
        if ch is None:
            self.on(eng, fn)
            return None
        self.owner.setdefault(ch.name, eng)
        self._burst[eng] = None
        v2 = ch.bump(k)
        self._raw(eng, lambda e, f=fn, s=ch.sem, k=k: f(e).then_inc(s, k))
        return [ch, v2]

    def dma(self, eng, out, in_, ch, **kw):
        self.owner.setdefault(ch.name, eng)
        tok = [ch, ch.bump(16)]
        b = self._burst[eng]
        if b is not None and b[0] is ch:
            b[1].append(tok)
            for t in b[1]:
                t[1] = ch.n
        else:
            self._burst[eng] = (ch, [tok])
        self._raw(eng, lambda e, o=out, i=in_, kw=kw, s=ch.sem:
                  e.dma_start(out=o, in_=i, **kw).then_inc(s, 16))
        return tok

    def barrier(self):
        ph = self.ch("ph")
        drains = {e: [] for e in ("y", "t", "v", "s", "g")}
        for name, eng in self.owner.items():
            if name.startswith("cc_"):
                continue
            c = self._chs[name]
            if c.n > 0:
                drains[eng].append((c.sem, c.n))
        for e_ in ("v", "s"):
            sc = self.selfch[e_]
            if sc.n > 0:
                drains[e_].append((sc.sem, sc.n))
        tgt = ph.n + 5
        ph.n = tgt
        for eng in ("y", "t", "v", "s", "g"):
            def f(e, ds=tuple(drains[eng]), s=ph.sem, t=tgt):
                for sem_, v_ in ds:
                    e.wait_ge(sem_, v_)
                e.nop().then_inc(s, 1)
                e.wait_ge(s, t)
            self.on(eng, f)


def build_nc():
    nc = bass.Bass("TRN2", target_bir_lowering=False, debug=False)
    k = K(nc)
    sv = k.selfch["v"]
    ss = k.selfch["s"]
    warm_ctx = {"ps": None, "ones": None}

    def warm():
        ps = warm_ctx["ps"]
        if ps is None:
            return
        on = warm_ctx["ones"]
        k.op("t", lambda e, ps=ps, on=on: e.matmul(ps[0:16, 0:16], on[0:1, 0:16],
                                                   on[0:1, 0:16], start=True, stop=True))

    x_in = nc.dram_tensor("x", [TOK, D], F32, kind="ExternalInput")
    wq_sl = nc.dram_tensor("wq_sl", [D, CIN_SH], F32, kind="ExternalInput")
    wk_sl = nc.dram_tensor("wk_sl", [D, CIN_SH], F32, kind="ExternalInput")
    wv_sl = nc.dram_tensor("wv_sl", [D, CIN_SH], F32, kind="ExternalInput")
    wf_sl = nc.dram_tensor("wf_sl", [CIN_SH, D], F32, kind="ExternalInput")
    bq_in = nc.dram_tensor("bq", [1, D], F32, kind="ExternalInput")
    bk_in = nc.dram_tensor("bk", [1, D], F32, kind="ExternalInput")
    bv_in = nc.dram_tensor("bv", [1, D], F32, kind="ExternalInput")
    bw_in = nc.dram_tensor("bw", [1, NH * 3], F32, kind="ExternalInput")
    out_d = nc.dram_tensor("out", [TOK, D], F32, kind="ExternalOutput")

    wqt_loc = nc.dram_tensor("wqt_loc", [CIN_SH, D], BF16)
    wkt_loc = nc.dram_tensor("wkt_loc", [CIN_SH, D], BF16)
    wvt_loc = nc.dram_tensor("wvt_loc", [CIN_SH, D], BF16)
    wf_loc = nc.dram_tensor("wf_loc", [CIN_SH, D], BF16)
    wqt = nc.dram_tensor("wqt", [D, D], BF16, addr_space="Shared")
    wkt = nc.dram_tensor("wkt", [D, D], BF16, addr_space="Shared")
    wvt = nc.dram_tensor("wvt", [D, D], BF16, addr_space="Shared")
    wf16 = nc.dram_tensor("wf16", [D, D], BF16, addr_space="Shared")
    h2t_d = nc.dram_tensor("h2t", [D, TOK], BF16)
    x1_d = nc.dram_tensor("x1", [TOK, D], F32)
    x2_d = nc.dram_tensor("x2", [TOK, D], F32)
    q_loc = nc.dram_tensor("q_loc", [D, TOK], BF16)
    k_loc = nc.dram_tensor("k_loc", [2 * D, HALF], BF16)
    v_loc = nc.dram_tensor("v_loc", [2 * NH * HALF, 256], BF16)
    k_full = [nc.dram_tensor("k_full%d" % i, [4 * D, HALF], BF16) for i in range(2)]
    v_full = [nc.dram_tensor("v_full%d" % i, [4 * NH * HALF, 256], BF16) for i in range(2)]
    kn_loc = nc.dram_tensor("kn_loc", [1, NH], F32)
    kn_g = nc.dram_tensor("kn_g", [4, NH], F32)

    ident16_h = nc.inline_tensor(np.eye(128, dtype=ml_dtypes.bfloat16), name="ident16")
    identf_h = nc.inline_tensor(np.eye(128, dtype=np.float32), name="identf")
    g_h = nc.inline_tensor(_g_const(), name="gconst")

    top = ExitStack()
    SBt = lambda n_, s_, d_=F32: top.enter_context(nc.sbuf_tensor(n_, s_, d_))
    ident16 = SBt("ident16s", [128, 128], BF16)
    identf = SBt("identfs", [128, 128], F32)
    ones_f = SBt("ones_f", [1, 128], F32)
    ones_bf = SBt("ones_bf", [1, 128], BF16)
    onescol_bf = SBt("onescol_bf", [128, 1], BF16)
    bias_q = SBt("bias_q", [128, NCH], F32)
    bias_k = SBt("bias_k", [128, NCH], F32)
    bv16 = SBt("bv16", [1, D], BF16)
    expb = SBt("expb", [128, NH], F32)
    epsc = SBt("epsc", [128, 1], F32)
    warm_ctx["ones"] = ones_bf

    ld_misc = k.ch("ld_misc")
    st_n = k.ch("st_n")
    mm = k.ch("mm")
    cc_wq = k.ch("cc_wq")
    cc_wk = k.ch("cc_wk")
    cc_wv = k.ch("cc_wv")
    cc_wf = k.ch("cc_wf")
    cc_k = [k.ch("cc_k0"), k.ch("cc_k1")]
    cc_v = [k.ch("cc_v0"), k.ch("cc_v1")]
    cc_n = k.ch("cc_n")

    with ExitStack() as p0:
        sbal = lambda n_, s_, d_=F32: p0.enter_context(nc.sbuf_tensor(n_, s_, d_))
        psal = lambda n_, s_, d_=F32: p0.enter_context(nc.psum_tensor(n_, s_, d_))
        g_sb = sbal("g_sb", [128, 4 * 128])
        m_sb = sbal("m_sb", [128, NH * 128])
        tmp_sb = sbal("tmp_sb", [128, 128])
        bwb = sbal("bwb", [128, 48])
        bqraw = sbal("bqraw", [128, NCH])
        bvrow = sbal("bvrow", [1, D])
        bwrow = sbal("bwrow", [1, 48])
        ps_bw = psal("ps_bw", [128, 64])
        ps_b = [psal("ps_b0", [128, 64]), psal("ps_b1", [128, 64])]

        k.dma("y", ident16[:], ident16_h.ap(), ld_misc)
        k.dma("y", identf[:], identf_h.ap(), ld_misc)
        k.dma("y", g_sb.ap().rearrange("p (c n) -> p c n", n=128),
              g_h.ap().rearrange("c p n -> p c n"), ld_misc)
        k.dma("y", bwrow[:], bw_in.ap(), ld_misc)
        k.dma("y", bvrow[:], bv_in.ap(), ld_misc)
        bqrow = sbal("bqrow", [1, D])
        bkrow = sbal("bkrow", [1, D])
        k.dma("y", bqrow[:], bq_in.ap(), ld_misc)
        tok_ld0 = k.dma("y", bkrow[:], bk_in.ap(), ld_misc)

        k.wait("v", tok_ld0)
        k.op("v", lambda e: e.memset(ones_f[:], 1.0))
        k.op("v", lambda e: e.memset(ones_bf[:], 1.0))
        k.op("v", lambda e: e.memset(onescol_bf[:], 1.0))
        k.op("v", lambda e: e.memset(epsc[:], float(EPS)))
        v0 = k.op("v", lambda e: e.tensor_copy(bv16[:], bvrow[:]))

        k.wait("t", tok_ld0)
        k.wait("t", v0)
        t0 = k.op("t", lambda e: e.matmul(ps_bw[:, 0:48], ones_f[:], bwrow[:],
                                          start=True, stop=True), mm)
        tcols = None
        for c in range(NCH):
            k.op("t", lambda e, c=c: e.matmul(
                ps_b[0].ap()[:, c:c + 1], bqrow.ap()[0:1, c * 128:(c + 1) * 128],
                ones_f.ap()[0:1, 0:1], start=True, stop=True))
            tcols = k.op("t", lambda e, c=c: e.matmul(
                ps_b[1].ap()[:, c:c + 1], bkrow.ap()[0:1, c * 128:(c + 1) * 128],
                ones_f.ap()[0:1, 0:1], start=True, stop=True), mm)
        k.wait("v", tcols)
        k.op("v", lambda e: e.tensor_copy(bqraw[:], ps_b[0].ap()[:, 0:NCH]))
        vbk = k.op("v", lambda e: e.tensor_copy(bias_k[:], ps_b[1].ap()[:, 0:NCH]))
        k.wait("v", t0)
        k.op("v", lambda e: e.tensor_copy(bwb[:, 0:48], ps_bw[:, 0:48]))
        vM = None
        for h in range(NH):
            mh = m_sb.ap()[:, h * 128:(h + 1) * 128]
            g0 = g_sb.ap()[:, 0:128]
            g1 = g_sb.ap()[:, 128:256]
            g2 = g_sb.ap()[:, 256:384]
            g3 = g_sb.ap()[:, 384:512]
            s0 = bwb.ap()[:, h * 3:h * 3 + 1]
            s1 = bwb.ap()[:, h * 3 + 1:h * 3 + 2]
            s2 = bwb.ap()[:, h * 3 + 2:h * 3 + 3]
            k.op("v", lambda e, g0=g0, s0=s0: e.tensor_scalar_mul(tmp_sb[:], g0, s0))
            k.op("v", lambda e, mh=mh, g3=g3: e.tensor_add(mh, tmp_sb[:], g3))
            k.op("v", lambda e, g1=g1, s1=s1: e.tensor_scalar_mul(tmp_sb[:], g1, s1))
            k.op("v", lambda e, mh=mh: e.tensor_add(mh, mh, tmp_sb[:]))
            k.op("v", lambda e, g2=g2, s2=s2: e.tensor_scalar_mul(tmp_sb[:], g2, s2))
            vM = k.op("v", lambda e, mh=mh: e.tensor_add(mh, mh, tmp_sb[:]))

        k.wait("t", vM)
        k.wait("t", vbk)
        sq_toks = [None, None]
        for c in range(NCH):
            h = c // 2
            pb = ps_b[c % 2]
            k.wait("t", sq_toks[c % 2])
            tq = k.op("t", lambda e, pb=pb, h=h, c=c: e.matmul(
                pb[:, 0:1], m_sb.ap()[:, h * 128:(h + 1) * 128], bqraw.ap()[:, c:c + 1],
                start=True, stop=True), mm)
            k.wait("s", tq)
            sq_toks[c % 2] = k.op("s", lambda e, pb=pb, c=c: e.copy(bias_q[:, c:c + 1],
                                                                    pb[:, 0:1]))

        ld_w = [k.ch("ld_w%d" % i) for i in range(4)]
        st_w = [k.ch("st_w%d" % i) for i in range(4)]
        ld_f = [k.ch("ld_f0"), k.ch("ld_f1")]
        st_f = [k.ch("st_f0"), k.ch("st_f1")]
        cons_w = k.ch("cons_w")
        wblk = [sbal("wblk%d" % i, [128, 1024]) for i in range(4)]
        wst = [p0.enter_context(nc.sbuf_tensor("wst%d" % i, [128, 1024], BF16))
               for i in range(2)]
        fblk = [sbal("fblk0", [128, D]), sbal("fblk1", [128, D])]
        fst = [p0.enter_context(nc.sbuf_tensor("fst0", [128, D], BF16)),
               p0.enter_context(nc.sbuf_tensor("fst1", [128, D], BF16))]
        ps_wp = [psal("ps_wp%d" % i, [128, 1024]) for i in range(2)]
        warm_ctx["ps"] = psal("ps_warm1", [16, 16])

        fev_toks = [None, None]
        fst_toks = [None, None]
        for j in range(CIN_SH // 128):
            b = j % 2
            k.wait("y", fev_toks[b])
            lw = k.dma("y", fblk[b][:], wf_sl.ap()[j * 128:(j + 1) * 128, :], ld_f[b])
            k.wait("v", lw)
            k.wait("v", fst_toks[b])
            fev_toks[b] = k.op("v", lambda e, b=b: e.tensor_copy(fst[b][:], fblk[b][:]))
            k.wait("g", fev_toks[b])
            fst_toks[b] = k.dma("g", wf_loc.ap()[j * 128:(j + 1) * 128, :], fst[b][:],
                                st_f[b])
        k.waitv("g", st_f[0], st_f[0].n)
        k.waitv("g", st_f[1], st_f[1].n)
        k.op("g", lambda e: e.collective_compute(
            "AllGather", ALU.bypass, replica_groups=R8,
            ins=[wf_loc.ap().opt()], outs=[wf16.ap().opt()]), cc_wf)

        idx = 0
        ev_toks = [None] * 2
        st_toks = [None] * 2
        for wsrc, wdst, rhs_is_m, ccx in ((wq_sl, wqt_loc, True, cc_wq),
                                          (wk_sl, wkt_loc, False, cc_wk),
                                          (wv_sl, wvt_loc, False, cc_wv)):
            srcw2 = wsrc.ap().rearrange("(cc two p) n -> p cc two n", p=128, two=2)
            for c2 in range(NCH // 2):
                bw = idx % 4
                bp = idx % 2
                if idx >= 4:
                    k.waitv("y", cons_w, 8 * (idx - 3))
                lw = k.dma("y", wblk[bw].ap().rearrange("p (two n) -> p two n", two=2),
                           srcw2[:, c2], ld_w[bw])
                k.wait("t", lw)
                k.wait("t", ev_toks[bp])
                tm = None
                for two in range(2):
                    c = 2 * c2 + two
                    rhs = (m_sb.ap()[:, (c // 2) * 128:(c // 2 + 1) * 128]
                           if rhs_is_m else identf.ap())
                    for j in range(CIN_SH // 128):
                        tm = k.op("t", lambda e, bw=bw, bp=bp, j=j, two=two,
                                  rhs=rhs: e.matmul(
                            ps_wp[bp].ap()[:, two * 512 + j * 128: two * 512 + (j + 1) * 128],
                            wblk[bw].ap()[:, two * 512 + j * 128: two * 512 + (j + 1) * 128],
                            rhs, start=True, stop=True), cons_w)
                k.wait("v", tm)
                k.wait("v", st_toks[bp])
                ev_toks[bp] = k.op("v", lambda e, bp=bp: e.tensor_copy(wst[bp][:],
                                                                       ps_wp[bp][:]))
                k.wait("g", ev_toks[bp])
                dstap = wdst.ap().rearrange("(j p) n -> p j n", p=128)
                srcap = wst[bp].ap().rearrange("p (two j n) -> p two j n", two=2, n=128)
                for two in range(2):
                    c = 2 * c2 + two
                    st_toks[bp] = k.dma("g", dstap[:, :, c * 128:(c + 1) * 128],
                                        srcap[:, two], st_w[bp])
                idx += 1
            for chx in st_w:
                k.waitv("g", chx, chx.n)
            k.op("g", lambda e, a_=wsrc, w1=wdst, w2={"wqt_loc": wqt, "wkt_loc": wkt,
                 "wvt_loc": wvt}[wdst.name]: e.collective_compute(
                "AllGather", ALU.bypass, replica_groups=R8,
                ins=[w1.ap().opt()], outs=[w2.ap().opt()]), ccx)

    k.barrier()

    ld_x = [k.ch("ld_x0"), k.ch("ld_x1b")]
    ld_wr = [k.ch("ld_wr0"), k.ch("ld_wr1")]
    ld_h2 = k.ch("ld_h2")
    st_h = k.ch("st_h")
    tr = k.ch("tr")
    ld_xs = [k.ch("ld_xs0"), k.ch("ld_xs1")]
    st_x1 = [k.ch("st_x10"), k.ch("st_x11")]
    mm_f = k.ch("mm_f")

    def ln_front_a(st, src_d, half, tt):
        xb, hb, stt_ = st["xb"], st["hb"], st["stats"]
        bx = tt % 2
        off = 8 * bx
        row = half * HALF + tt * 128
        k.wait("y", st["xb_free"][bx])
        lw = k.dma("y", xb[bx][:], src_d.ap()[row:row + 128, :], ld_x[bx])
        k.wait("s", lw)
        k.wait("s", st["hb_free"][bx])
        st["a1"] = k.op("s", lambda e, bx=bx, off=off: e.activation(
            hb[bx][:], xb[bx][:], AF.Square, accum_out=stt_[:, off:off + 1]))

    def ln_front_b(st, src_d, half, tt):
        xb, hb, stt_ = st["xb"], st["hb"], st["stats"]
        bx = tt % 2
        off = 8 * bx
        a1 = st["a1"]
        k.wait("v", a1)
        k.op("v", lambda e, bx=bx, off=off: e.reduce_sum(stt_[:, off + 1:off + 2],
                                                         xb[bx][:], axis=AX.X))
        k.op("v", lambda e, off=off: e.tensor_scalar_mul(stt_[:, off + 2:off + 3],
                                                         stt_[:, off + 1:off + 2], 1.0 / D))
        k.op("v", lambda e, off=off: e.tensor_scalar_mul(stt_[:, off + 3:off + 4],
                                                         stt_[:, off:off + 1], 1.0 / D))
        k.op("v", lambda e, off=off: e.tensor_mul(stt_[:, off + 4:off + 5],
                                                  stt_[:, off + 2:off + 3],
                                                  stt_[:, off + 2:off + 3]))
        d1 = k.op("v", lambda e, off=off: e.tensor_sub(stt_[:, off + 3:off + 4],
                                                       stt_[:, off + 3:off + 4],
                                                       stt_[:, off + 4:off + 5]))
        k.wait("s", d1)
        k.op("s", lambda e, off=off: e.activation(stt_[:, off + 5:off + 6],
                                                  stt_[:, off + 3:off + 4], AF.Ln,
                                                  bias=epsc[:, 0:1]))
        a2 = k.op("s", lambda e, off=off: e.activation(stt_[:, off + 6:off + 7],
                                                       stt_[:, off + 5:off + 6], AF.Exp,
                                                       scale=-0.5))
        k.wait("v", a2)
        k.op("v", lambda e, off=off: e.tensor_mul(stt_[:, off + 7:off + 8],
                                                  stt_[:, off + 2:off + 3],
                                                  stt_[:, off + 6:off + 7]))
        d2 = k.op("v", lambda e, off=off: e.tensor_scalar_mul(stt_[:, off + 7:off + 8],
                                                              stt_[:, off + 7:off + 8],
                                                              -1.0))
        k.wait("s", d2)
        a3 = k.op("s", lambda e, bx=bx, off=off: e.activation(
            hb[bx][:], xb[bx][:], AF.Identity,
            bias=stt_[:, off + 7:off + 8], scale=stt_[:, off + 6:off + 7]))
        st["xb_free"][bx] = a3
        st["a3"][bx] = a3

    def ln_back(st, half, tt, h1T, dst_dram):
        hb, h2st, ps_tr = st["hb"], st["h2st"], st["ps_tr"]
        bx = tt % 2
        k.wait("t", st["a3"][bx])
        for c in range(NCH):
            pb = c % 4
            k.wait("t", st["ps_tr_free"][pb])
            tv = k.op("t", lambda e, pb=pb, bx=bx, c=c: e.matmul(
                ps_tr[pb][:], hb[bx].ap()[:, c * 128:(c + 1) * 128], ident16[:],
                start=True, stop=True), tr)
            eng = "v" if c % 4 < 3 else "s"
            k.wait(eng, tv)
            if h1T is not None:
                dst = h1T.ap()[:, c * HALF + tt * 128: c * HALF + tt * 128 + 128]
            else:
                k.wait(eng, st["h2st_free"])
                dst = h2st.ap()[:, c * 128:(c + 1) * 128]
            if eng == "v":
                ctok = k.op("v", lambda e, dst=dst, pb=pb: e.tensor_copy(dst,
                                                                         ps_tr[pb][:]))
            else:
                ctok = k.op("s", lambda e, dst=dst, pb=pb: e.copy(dst, ps_tr[pb][:]))
            st["ps_tr_free"][pb] = ctok
            st["cp_last"][eng] = ctok
        st["hb_free"][bx] = (tr, tr.n)
        if h1T is None:
            k.wait("g", st["cp_last"]["v"])
            k.wait("g", st["cp_last"]["s"])
            col = half * HALF + tt * 128
            dstap = dst_dram.ap().rearrange("(c p) m -> p c m", p=128)
            srcap = h2st.ap().rearrange("p (c n) -> p c n", n=128)
            htok = None
            for q4 in range(4):
                htok = k.dma("g", dstap[:, q4 * 8:(q4 + 1) * 8, col:col + 128],
                             srcap[:, q4 * 8:(q4 + 1) * 8, :], st_h)
            st["h2st_free"] = htok
        st["ln_i"] += 1

    def ln_pass(st, src_d, half, h1T=None, dst_dram=None):
        for tt in range(9):
            if tt < 8:
                ln_front_a(st, src_d, half, tt)
            if tt >= 1:
                ln_back(st, half, tt - 1, h1T, dst_dram)
            if tt < 8:
                ln_front_b(st, src_d, half, tt)

    def ffn_half(st, xin_d, xout_d, half, wsrc, first_use_wait=None):
        wres, xsl, x1st, ps_f, h1T = st["wres"], st["xsl"], st["x1st"], st["ps_f"], st["h1T"]
        if first_use_wait is not None:
            k.wait("y", first_use_wait)
        k.waitv("t", sv, sv.n)
        k.waitv("t", ss, ss.n)
        for co in range(8):
            b = co % 2
            k.wait("y", st["wres_free"][b])
            srcap = wsrc.ap().rearrange("(j p) n -> p j n", p=128)
            dstap = wres[b].ap().rearrange("p (j n) -> p j n", n=512)
            lw = None
            for q4 in range(4):
                lw = k.dma("y", dstap[:, q4 * 8:(q4 + 1) * 8, :],
                           srcap[:, q4 * 8:(q4 + 1) * 8, co * 512:(co + 1) * 512], ld_wr[b])
            k.wait("t", lw)
            for tt in range(8):
                gi = st["ffn_i"] * 8 + tt
                pb = gi % 3
                k.wait("t", st["psf_free"][pb])
                tmf = None
                for j in range(NCH):
                    tmf = k.op("t", lambda e, pb=pb, b=b, j=j, tt=tt: e.matmul(
                        ps_f[pb][:],
                        h1T.ap()[:, j * HALF + tt * 128: j * HALF + tt * 128 + 128],
                        wres[b].ap()[:, j * 512:(j + 1) * 512],
                        start=(j == 0), stop=(j == NCH - 1)),
                        mm_f if j == NCH - 1 else None)
                bx = gi % 2
                k.wait("y", st["xsl_free"][bx])
                row = half * HALF + tt * 128
                lx = k.dma("y", xsl[bx][:],
                           xin_d.ap()[row:row + 128, co * 512:(co + 1) * 512], ld_xs[bx])
                k.wait("v", tmf)
                k.wait("v", lx)
                k.wait("v", st["x1st_free"][bx])
                ev = k.op("v", lambda e, pb=pb, bx=bx: e.scalar_tensor_tensor(
                    x1st[bx][:], ps_f[pb][:], 0.5, xsl[bx][:], ALU.mult, ALU.add))
                st["psf_free"][pb] = ev
                st["xsl_free"][bx] = ev
                k.wait("g", ev)
                st["x1st_free"][bx] = k.dma(
                    "g", xout_d.ap()[row:row + 128, co * 512:(co + 1) * 512],
                    x1st[bx][:], st_x1[bx])
            st["wres_free"][b] = (mm_f, mm_f.n)
            st["ffn_i"] += 1

    def ffn_phase(xin_d, xout_d, ln_out_dram, wsrc, first_use_wait, tag):
        with ExitStack() as px:
            sbal = lambda n_, s_, d_=F32: px.enter_context(nc.sbuf_tensor(tag + n_, s_, d_))
            st = {
                "xb": [sbal("xb0", [128, D]), sbal("xb1", [128, D])],
                "hb": [sbal("hb0", [128, D], BF16), sbal("hb1", [128, D], BF16)],
                "stats": sbal("stats", [128, 16]),
                "h2st": sbal("h2st", [128, NCH * 128], BF16),
                "h1T": sbal("h1T", [128, NCH * HALF], BF16),
                "wres": [sbal("wres0", [128, NCH * 512], BF16),
                         sbal("wres1", [128, NCH * 512], BF16)],
                "xsl": [sbal("xsl0", [128, 512]), sbal("xsl1", [128, 512])],
                "x1st": [sbal("x1st0", [128, 512]), sbal("x1st1", [128, 512])],
                "ps_tr": [px.enter_context(nc.psum_tensor(tag + "ps_tr%d" % i,
                                                           [128, 128], F32))
                          for i in range(4)],
                "ps_f": [px.enter_context(nc.psum_tensor(tag + "ps_f%d" % i,
                                                          [128, 512], F32))
                         for i in range(3)],
                "ps_warm": px.enter_context(nc.psum_tensor(tag + "ps_warm", [16, 16], F32)),
                "ln_i": 0, "ffn_i": 0,
                "xb_free": [None, None], "hb_free": [None, None],
                "psf_free": [None, None, None], "xsl_free": [None, None],
                "x1st_free": [None, None], "wres_free": [None, None],
                "ps_tr_free": [None, None, None, None], "h2st_free": None,
                "a3": [None, None],
                "cp_last": {"v": None, "s": None},
            }
            warm_ctx["ps"] = st["ps_warm"]
            for half in (0, 1):
                ln_pass(st, xin_d, half, h1T=st["h1T"])
                ffn_half(st, xin_d, xout_d, half, wsrc,
                         first_use_wait=first_use_wait if st["ffn_i"] == 0 else None)
                if ln_out_dram is not None:
                    k.waitv("y", st_x1[0], st_x1[0].n)
                    k.waitv("y", st_x1[1], st_x1[1].n)
                    ln_pass(st, xout_d, half, h1T=None, dst_dram=ln_out_dram)
            k.barrier()

    ffn_phase(x_in, x1_d, h2t_d, wf16, first_use_wait=(cc_wf, 1), tag="p2_")

    ld_w2 = [k.ch("ld_w2%d" % i) for i in range(6)]
    mm_k = k.ch("mm_k")
    st_kv = [k.ch("st_kv0"), k.ch("st_kv1")]
    mm_n = k.ch("mm_n")
    ld_n = [k.ch("ld_n0"), k.ch("ld_n1")]
    ld_kn = k.ch("ld_kn")

    with ExitStack() as p3:
        sbal = lambda n_, s_, d_=F32: p3.enter_context(nc.sbuf_tensor(n_, s_, d_))
        h2res = p3.enter_context(nc.sbuf_tensor("h2res", [128, NCH * HALF], BF16))
        wslab = [p3.enter_context(nc.sbuf_tensor("wslab%d" % i, [128, NCH * 256], BF16))
                 for i in range(6)]
        kst = [p3.enter_context(nc.sbuf_tensor("kst0", [128, 512], BF16)),
               p3.enter_context(nc.sbuf_tensor("kst1", [128, 512], BF16))]
        sqs = [p3.enter_context(nc.sbuf_tensor("sqs0", [128, 512], BF16)),
               p3.enter_context(nc.sbuf_tensor("sqs1", [128, 512], BF16))]
        ntmp = sbal("ntmp", [1, 8])
        nrm = sbal("nrm", [16, 8])
        qmaxr = sbal("qmaxr", [1, NH])
        kmaxr = sbal("kmaxr", [1, NH])
        knsb = sbal("knsb", [16, 4])
        nb1 = sbal("nb1", [1, NH])
        ps_kv = [p3.enter_context(nc.psum_tensor("ps_kv%d" % i, [128, 512], F32))
                 for i in range(3)]
        ps_n = [p3.enter_context(nc.psum_tensor("ps_n%d" % i, [1, 512], F32))
                for i in range(4)]
        ps_x = p3.enter_context(nc.psum_tensor("ps_x", [128, 64], F32))
        warm_ctx["ps"] = None

        st3 = {"wslab_free": [None] * 6, "pskv_free": [None, None, None],
               "kst_free": [None, None], "wi": 0, "pre": {}, "pv": 0,
               "psn_free": [None, None, None, None], "nrm_last": None,
               "pskv3": True,
               "sq_free": [None, None]}

        def load_h2res(half):
            srcap = h2t_d.ap().rearrange("(j p) m -> p j m", p=128)
            dstap = h2res.ap().rearrange("p (j m) -> p j m", m=HALF)
            lw = None
            for q4 in range(4):
                lw = k.dma("y", dstap[:, q4 * 8:(q4 + 1) * 8, :],
                           srcap[:, q4 * 8:(q4 + 1) * 8, half * HALF:(half + 1) * HALF],
                           ld_h2)
            return lw

        def slab_load_kq(wsrc, c):
            b = st3["wi"] % 6
            st3["wi"] += 1
            srcw = wsrc.ap().rearrange("(j p) n -> p j n", p=128)
            k.wait("y", st3["wslab_free"][b])
            dstw = wslab[b].ap().rearrange("p (j n) -> p j n", n=256)[:, :, 0:128]
            lw = None
            for q4 in range(4):
                lw = k.dma("y", dstw[:, q4 * 8:(q4 + 1) * 8, :],
                           srcw[:, q4 * 8:(q4 + 1) * 8, c * 128:(c + 1) * 128], ld_w2[b])
            return b, lw

        def slab_load_v(h):
            b = st3["wi"] % 6
            st3["wi"] += 1
            srcw = wvt.ap().rearrange("(j p) n -> p j n", p=128)
            k.wait("y", st3["wslab_free"][b])
            dstw = wslab[b].ap().rearrange("p (j n) -> p j n", n=256)
            lw = None
            for q4 in range(4):
                lw = k.dma("y", dstw[:, q4 * 8:(q4 + 1) * 8, :],
                           srcw[:, q4 * 8:(q4 + 1) * 8, h * 256:(h + 1) * 256], ld_w2[b])
            return b, lw

        def kq_pass(half, wsrc, bias_sb, dst_dram, h2w, maxrow, split):
            for c in range(NCH):
                if ("kq", half, c) in st3["pre"]:
                    b, lw = st3["pre"].pop(("kq", half, c))
                else:
                    b, lw = slab_load_kq(wsrc, c)
                k.wait("t", lw)
                k.wait("t", h2w)
                sq_toks = [None, None]
                for t2 in range(2):
                    pb = st3["pv"] % 3
                    st3["pv"] += 1
                    k.wait("t", st3["pskv_free"][pb])
                    tmk = None
                    for j in range(NCH):
                        tmk = k.op("t", lambda e, pb=pb, b=b, j=j, t2=t2: e.matmul(
                            ps_kv[pb][:], wslab[b].ap()[:, j * 256:j * 256 + 128],
                            h2res.ap()[:, j * HALF + t2 * 512: j * HALF + t2 * 512 + 512],
                            start=(j == 0), stop=(j == NCH - 1)),
                            mm_k if j == NCH - 1 else None)
                    gb = t2
                    k.wait("s", tmk)
                    k.wait("s", st3["kst_free"][gb])
                    ev = k.op("s", lambda e, pb=pb, c=c, gb=gb, bias_sb=bias_sb:
                              e.activation(kst[gb][:], ps_kv[pb][:], AF.Identity,
                                           bias=bias_sb.ap()[:, c:c + 1]))
                    st3["pskv_free"][pb] = ev
                    k.wait("g", ev)
                    if split:
                        ro = half * D + c * 128
                        col = t2 * 512
                    else:
                        ro = c * 128
                        col = half * HALF + t2 * 512
                    st3["kst_free"][gb] = k.dma(
                        "g", dst_dram.ap()[ro:ro + 128, col:col + 512],
                        kst[gb][:], st_kv[gb])
                    k.wait("s", st3["sq_free"][gb])
                    sq_toks[t2] = k.op("s", lambda e, gb=gb: e.activation(
                        sqs[gb][:], kst[gb][:], AF.Square))
                for t2 in range(2):
                    g_ = 2 * half + t2
                    k.wait("t", sq_toks[t2])
                    if c % 2 == 0:
                        k.wait("t", st3["psn_free"][g_])
                    tn = k.op("t", lambda e, g_=g_, t2=t2, c=c: e.matmul(
                        ps_n[g_][:], onescol_bf[:], sqs[t2].ap()[:, 0:512],
                        start=(c % 2 == 0), stop=(c % 2 == 1)), mm_k)
                    st3["sq_free"][t2] = (mm_k, mm_k.n)
                    if c % 2 == 1:
                        h = c // 2
                        k.wait("v", tn)
                        k.op("v", lambda e, g_=g_: e.reduce_max(ntmp.ap()[0:1, g_:g_ + 1],
                                                                ps_n[g_][:], axis=AX.X))
                        vt = k.op("v", lambda e, g_=g_, h=h, maxrow=maxrow: e.tensor_max(
                            maxrow.ap()[0:1, h:h + 1], maxrow.ap()[0:1, h:h + 1],
                            ntmp.ap()[0:1, g_:g_ + 1]))
                        st3["psn_free"][g_] = vt
                        st3["nrm_last"] = vt
                st3["wslab_free"][b] = (mm_k, mm_k.n)

        def v_pass(half, h2w):
            k.waitv("v", ss, ss.n)
            for h in range(NH):
                if ("v", half, h) in st3["pre"]:
                    b, lw = st3["pre"].pop(("v", half, h))
                else:
                    b, lw = slab_load_v(h)
                k.wait("t", lw)
                k.wait("t", h2w)
                for tt in range(8):
                    pb = st3["pv"] % 3
                    st3["pv"] += 1
                    k.wait("t", st3["pskv_free"][pb])
                    for j in range(NCH):
                        k.op("t", lambda e, pb=pb, b=b, j=j, tt=tt: e.matmul(
                            ps_kv[pb][:, 0:256],
                            h2res.ap()[:, j * HALF + tt * 128: j * HALF + tt * 128 + 128],
                            wslab[b].ap()[:, j * 256:(j + 1) * 256],
                            start=(j == 0), stop=False))
                    tmk = k.op("t", lambda e, pb=pb, h=h: e.matmul(
                        ps_kv[pb][:, 0:256], ones_bf[:], bv16.ap()[:, h * 256:(h + 1) * 256],
                        start=False, stop=True), mm_k)
                    gb = tt % 2
                    k.wait("v", tmk)
                    k.wait("v", st3["kst_free"][gb])
                    ev = k.op("v", lambda e, pb=pb, gb=gb: e.tensor_copy(
                        kst[gb][:, 0:256], ps_kv[pb][:, 0:256]))
                    st3["pskv_free"][pb] = ev
                    k.wait("g", ev)
                    row = half * NH * HALF + h * HALF + tt * 128
                    st3["kst_free"][gb] = k.dma("g", v_loc.ap()[row:row + 128, :],
                                                kst[gb][:, 0:256], st_kv[gb])
                st3["wslab_free"][b] = (mm_k, mm_k.n)

        mz = k.op("v", lambda e: e.memset(kmaxr[:], 0.0))
        mz = k.op("v", lambda e: e.memset(qmaxr[:], 0.0))
        for half in (0, 1):
            if half == 1:
                k.waitv("y", mm_k, mm_k.n)
            h2w = load_h2res(half)
            if half == 0:
                k.waitv("y", cc_wk, 1)
                k.waitv("y", cc_wv, 1)
            kq_pass(half, wkt, bias_k, k_loc, h2w, kmaxr, split=True)
            k.waitv("g", st_kv[0], st_kv[0].n)
            k.waitv("g", st_kv[1], st_kv[1].n)
            k.op("g", lambda e, half=half: e.collective_compute(
                "AllGather", ALU.bypass, replica_groups=R4,
                ins=[k_loc.ap()[half * D:(half + 1) * D, :].opt()],
                outs=[k_full[half].ap().opt()]), cc_k[half])
        ktok = st3["nrm_last"]
        k.wait("g", ktok)
        kstn = k.dma("g", kn_loc.ap(), kmaxr[:], st_n)
        k.wait("g", kstn)
        k.op("g", lambda e: e.collective_compute(
            "AllGather", ALU.bypass, replica_groups=R4,
            ins=[kn_loc.ap().opt()], outs=[kn_g.ap().opt()]), cc_n)

        for half in (0, 1):
            if half == 0:
                for h in range(3):
                    st3["pre"][("v", 0, h)] = slab_load_v(h)
            k.waitv("y", mm_k, mm_k.n)
            h2w = load_h2res(half)
            v_pass(half, h2w)
            k.waitv("g", st_kv[0], st_kv[0].n)
            k.waitv("g", st_kv[1], st_kv[1].n)
            k.op("g", lambda e, half=half: e.collective_compute(
                "AllGather", ALU.bypass, replica_groups=R4,
                ins=[v_loc.ap()[half * NH * HALF:(half + 1) * NH * HALF, :].opt()],
                outs=[v_full[half].ap().opt()]), cc_v[half])

        for half in (0, 1):
            if half == 0:
                k.waitv("y", cc_wq, 1)
                for c in range(3):
                    st3["pre"][("kq", 0, c)] = slab_load_kq(wqt, c)
            k.waitv("y", mm_k, mm_k.n)
            h2w = load_h2res(half)
            kq_pass(half, wqt, bias_q, q_loc, h2w, qmaxr, split=False)
        qtok = st3["nrm_last"]

        k.waitv("y", cc_n, 1)
        kln = None
        for r in range(4):
            kln = k.dma("y", knsb[:, r:r + 1],
                        kn_g.ap()[r:r + 1, :].rearrange("one h -> h one"), ld_kn)

        k.wait("v", kln)
        k.op("v", lambda e: e.reduce_max(nrm.ap()[0:16, 4:5], knsb[:], axis=AX.X))
        k.wait("t", qtok)
        tq1 = k.op("t", lambda e: e.transpose(ps_x[0:16, 0:1], qmaxr[:],
                                              identf.ap()[0:1, 0:1]), mm)
        k.wait("v", tq1)
        k.op("v", lambda e: e.tensor_copy(nrm.ap()[0:16, 5:6], ps_x[0:16, 0:1]))
        d4 = k.op("v", lambda e: e.tensor_mul(nrm.ap()[0:16, 6:7], nrm.ap()[0:16, 4:5],
                                              nrm.ap()[0:16, 5:6]))
        k.wait("s", d4)
        a4 = k.op("s", lambda e: e.activation(nrm.ap()[0:16, 7:8], nrm.ap()[0:16, 6:7],
                                              AF.Sqrt))
        k.wait("v", a4)
        d5 = k.op("v", lambda e: e.tensor_scalar_mul(nrm.ap()[0:16, 7:8],
                                                     nrm.ap()[0:16, 7:8], -1.0))
        k.wait("t", d5)
        tq2 = k.op("t", lambda e: e.transpose(ps_x[0:1, 0:16], nrm.ap()[0:16, 7:8],
                                              identf.ap()[0:16, 0:16]), mm)
        k.wait("v", tq2)
        d6 = k.op("v", lambda e: e.tensor_copy(nb1[:], ps_x[0:1, 0:16]))
        k.wait("t", d6)
        tq3 = k.op("t", lambda e: e.matmul(ps_x[:, 0:16], ones_f[:], nb1[:],
                                           start=True, stop=True), mm)
        k.wait("v", tq3)
        k.op("v", lambda e: e.tensor_copy(expb[:], ps_x[:, 0:16]))
    k.barrier()

    ld_k4 = [k.ch("ld_k40"), k.ch("ld_k41")]
    ld_v4 = [k.ch("ld_v40"), k.ch("ld_v41")]
    ld_x1 = [k.ch("ld_x1%d" % i) for i in range(4)]
    st_x2 = [k.ch("st_x2%d" % i) for i in range(4)]
    tl = k.ch("tl")
    av = k.ch("av")

    with ExitStack() as p4:
        kh = [p4.enter_context(nc.sbuf_tensor("kh0", [128, 2 * LK], BF16)),
              p4.enter_context(nc.sbuf_tensor("kh1", [128, 2 * LK], BF16))]
        vh = [p4.enter_context(nc.sbuf_tensor("vh0", [128, MCH * 257], BF16)),
              p4.enter_context(nc.sbuf_tensor("vh1", [128, MCH * 257], BF16))]
        qh = [p4.enter_context(nc.sbuf_tensor("qh0", [128, 2 * TOK], BF16)),
              p4.enter_context(nc.sbuf_tensor("qh1", [128, 2 * TOK], BF16))]
        p_sb = [p4.enter_context(nc.sbuf_tensor("p0", [128, 1024], BF16)),
                p4.enter_context(nc.sbuf_tensor("p1", [128, 1024], BF16))]
        x1sl = [p4.enter_context(nc.sbuf_tensor("x1sl%d" % i, [128, 256], F32))
                for i in range(4)]
        x2st = [p4.enter_context(nc.sbuf_tensor("x2st%d" % i, [128, 256], F32))
                for i in range(4)]
        rcp = p4.enter_context(nc.sbuf_tensor("rcp", [128, 4], F32))
        ps_l = [p4.enter_context(nc.psum_tensor("ps_l0", [128, 1024], F32)),
                p4.enter_context(nc.psum_tensor("ps_l1", [128, 1024], F32))]
        ps_av4 = p4.enter_context(nc.psum_tensor("ps_av4", [128, 2048], F32))
        ps_av = [ps_av4.ap()[:, i * 512:(i + 1) * 512] for i in range(4)]

        k.op("v", lambda e: e.memset(
            vh[0].ap().rearrange("p (c n) -> p c n", n=257)[:, :, 256:257], 1.0))
        vones = k.op("v", lambda e: e.memset(
            vh[1].ap().rearrange("p (c n) -> p c n", n=257)[:, :, 256:257], 1.0))

        def load_head(h, hb):
            ktok_ = None
            for fc in range(2):
                for r in range(4):
                    for hf in range(2):
                        ktok_ = k.dma(
                            "y",
                            kh[hb][:, fc * LK + r * TOK + hf * HALF:
                                   fc * LK + r * TOK + (hf + 1) * HALF],
                            k_full[hf].ap()[r * D + h * 256 + fc * 128:
                                            r * D + h * 256 + fc * 128 + 128, :],
                            ld_k4[hb])
            for fc in range(2):
                ktok_ = k.dma("y", qh[hb][:, fc * TOK:(fc + 1) * TOK],
                              q_loc.ap()[h * 256 + fc * 128: h * 256 + fc * 128 + 128, :],
                              ld_k4[hb])
            vtok_ = None
            for r in range(4):
                for hf in range(2):
                    for s2 in range(2):
                        base = (r * NH + h) * HALF + s2 * 512
                        cbase = r * 16 + hf * 8 + s2 * 4
                        dst = vh[hb].ap().rearrange("p (c n) -> p c n", n=257)[
                            :, cbase:cbase + 4, 0:256]
                        src = v_full[hf].ap()[base:base + 512, :].rearrange(
                            "(c p) n -> p c n", p=128)
                        vtok_ = k.dma("g", dst, src, ld_v4[hb])
            return ktok_, vtok_

        for hf in range(2):
            k.waitv("y", cc_k[hf], 1)
            k.waitv("y", cc_v[hf], 1)
            k.waitv("g", cc_k[hf], 1)
            k.waitv("g", cc_v[hf], 1)
        k.wait("g", vones)
        marks = {0: load_head(0, 0)}
        x1_free = [None] * 4
        x2_free = [None] * 4
        exp_toks = {}
        ev4_last = None

        mc_glob = 0
        for h in range(NH):
            hb = h % 2
            k.wait("t", marks[h][0])
            k.wait("t", marks[h][1])
            if h == 0:
                k.wait("t", vones)
            if h + 1 < NH:
                if h >= 1:
                    k.waitv("y", av, 128 * h)
                    k.waitv("g", av, 128 * h)
                marks[h + 1] = load_head(h + 1, (h + 1) % 2)
            for lg in range(4):
                r0 = mc_glob // 2
                NR = MCH // 2
                for r in range(NR):
                    ri = r0 + r
                    pb = ri % 2
                    k.wait("t", exp_toks.pop(ri - 2, None))
                    for jj in range(2):
                        j = 2 * r + jj
                        for fc in range(2):
                            k.op("t", lambda e, pb=pb, hb=hb, fc=fc, j=j, jj=jj,
                                 lg=lg: e.matmul(
                                ps_l[pb].ap()[:, jj * 512:(jj + 1) * 512],
                                kh[hb].ap()[:, fc * LK + j * 128: fc * LK + j * 128 + 128],
                                qh[hb].ap()[:, fc * TOK + lg * 512: fc * TOK + lg * 512 + 512],
                                start=(fc == 0), stop=(fc == 1)),
                                tl if (fc == 1 and jj == 1) else None)
                    if r >= 1:
                        k.wait("t", exp_toks.get(ri - 1))
                        if r == 1:
                            k.wait("t", ev4_last)
                        for jj in range(2):
                            jm = 2 * (r - 1) + jj
                            for lc in range(4):
                                k.op("t", lambda e, lc=lc, hb=hb, jm=jm, jj=jj,
                                     pp=(ri - 1) % 2: e.matmul(
                                    ps_av[lc][:, 0:257],
                                    p_sb[pp].ap()[:, jj * 512 + lc * 128: jj * 512 + (lc + 1) * 128],
                                    vh[hb].ap()[:, jm * 257:(jm + 1) * 257],
                                    start=(jm == 0), stop=False),
                                    av if (lc == 3 and jj == 1) else None)
                    k.waitv("s", tl, ri + 1)
                    k.waitv("s", av, max(0, ri - 1))
                    exp_toks[ri] = k.op("s", lambda e, pb=pb, h=h: e.activation(
                        p_sb[pb][:], ps_l[pb][:], AF.Exp, bias=expb.ap()[:, h:h + 1]))
                k.wait("t", exp_toks.get(r0 + NR - 1))
                for jj in range(2):
                    jm = MCH - 2 + jj
                    for lc in range(4):
                        k.op("t", lambda e, lc=lc, hb=hb, jm=jm, jj=jj,
                             pp=(r0 + NR - 1) % 2: e.matmul(
                            ps_av[lc][:, 0:257],
                            p_sb[pp].ap()[:, jj * 512 + lc * 128: jj * 512 + (lc + 1) * 128],
                            vh[hb].ap()[:, jm * 257:(jm + 1) * 257],
                            start=False, stop=(jm == MCH - 1)),
                            av if (lc == 3 and jj == 1) else None)
                mc_glob += MCH
                for lc in range(4):
                    bx = lc
                    row = lg * 512 + lc * 128
                    k.wait("y", x1_free[bx])
                    lx = k.dma("y", x1sl[bx][:],
                               x1_d.ap()[row:row + 128, h * 256:(h + 1) * 256], ld_x1[bx])
                    if lc == 0:
                        k.waitv("v", av, av.n)
                        k.op("v", lambda e: e.reciprocal(
                            rcp[:], ps_av4.ap().rearrange("p (lc n) -> p lc n", n=512)[
                                :, :, 256:257].rearrange("p lc one -> p (lc one)")))
                    k.wait("v", lx)
                    k.wait("v", x2_free[bx])
                    ev = k.op("v", lambda e, lc=lc, bx=bx: e.scalar_tensor_tensor(
                        x2st[bx][:], ps_av[lc][:, 0:256], rcp[:, lc:lc + 1], x1sl[bx][:],
                        ALU.mult, ALU.add))
                    x1_free[bx] = ev
                    ev4_last = ev
                    k.wait("g", ev)
                    x2_free[bx] = k.dma("g", x2_d.ap()[row:row + 128,
                                                       h * 256:(h + 1) * 256],
                                        x2st[bx][:], st_x2[bx])
    k.barrier()

    ffn_phase(x2_d, out_d, None, wf16, first_use_wait=None, tag="p5_")

    k.waitv("g", st_x1[0], st_x1[0].n)
    k.waitv("g", st_x1[1], st_x1[1].n)
    k.waitv("y", st_x1[0], st_x1[0].n)
    k.waitv("y", st_x1[1], st_x1[1].n)

    with nc.Block() as block:
        @block.sync
        def _(e):
            for f in k.rec["y"]:
                f(e)

        @block.tensor
        def _(e):
            for f in k.rec["t"]:
                f(e)

        @block.vector
        def _(e):
            for f in k.rec["v"]:
                f(e)

        @block.scalar
        def _(e):
            for f in k.rec["s"]:
                f(e)

        @block.gpsimd
        def _(e):
            for f in k.rec["g"]:
                f(e)

    top.close()
    return nc


def kernel(L0_attn_x, L0_attn_Wq, L0_attn_bq, L0_attn_Wk, L0_attn_bk,
           L0_attn_Wv, L0_attn_bv, L0_attn_bias_weight, L0_ffn_weight):
    if "nc" not in _CACHE:
        _CACHE["nc"] = build_nc()
    nc = _CACHE["nc"]

    x = np.ascontiguousarray(np.asarray(L0_attn_x, dtype=np.float32)).reshape(16384, D)
    Wq = np.asarray(L0_attn_Wq, dtype=np.float32)
    Wk = np.asarray(L0_attn_Wk, dtype=np.float32)
    Wv = np.asarray(L0_attn_Wv, dtype=np.float32)
    Wf = np.asarray(L0_ffn_weight, dtype=np.float32)
    bq = np.asarray(L0_attn_bq, dtype=np.float32).reshape(1, D)
    bk = np.asarray(L0_attn_bk, dtype=np.float32).reshape(1, D)
    bv = np.asarray(L0_attn_bv, dtype=np.float32).reshape(1, D)
    bw = np.asarray(L0_attn_bias_weight, dtype=np.float32).reshape(1, NH * 3)

    in_maps = []
    for i in range(NCORES):
        sl = slice(i * CIN_SH, (i + 1) * CIN_SH)
        in_maps.append({
            "x": np.ascontiguousarray(x[i * TOK:(i + 1) * TOK]),
            "wq_sl": np.ascontiguousarray(Wq[:, sl]),
            "wk_sl": np.ascontiguousarray(Wk[:, sl]),
            "wv_sl": np.ascontiguousarray(Wv[:, sl]),
            "wf_sl": np.ascontiguousarray(Wf[sl, :]),
            "bq": bq, "bk": bk, "bv": bv, "bw": bw,
        })

    res = run_bass_kernel_spmd(nc, in_maps, core_ids=list(range(NCORES)))
    out = np.concatenate([r["out"] for r in res.results], axis=0)
    return out.reshape(2, 8192, D)
